# version 02
# baseline (speedup 1.0000x reference)
import numpy as np
import ml_dtypes
from contextlib import ExitStack

import concourse.bass as bass
import concourse.mybir as mybir
from concourse.bass_utils import run_bass_kernel_spmd

F32 = mybir.dt.float32
BF16 = mybir.dt.bfloat16
AF = mybir.ActivationFunctionType
ALU = mybir.AluOpType
AX = mybir.AxisListType

D = 2048
NH = 32
DS = 64
HD = 64
CONVD = 6144
PROJ = 8224
HID = 8192
NCORE = 8
C23 = float(1.5 * 2.0**23)
C52 = float(1.5 * 2.0**52)
NEGBIG = -1e30
XOFF = 16

ENGS = ("sync", "scalar", "vector", "tensor", "gpsimd")
QSEM = {"sync": "dS", "scalar": "dA", "gpsimd": "dG"}


def spans(lo, hi, step):
    return [(s, min(s + step, hi)) for s in range(lo, hi, step)]


class Sched:

    def __init__(self):
        self.q = {e: [] for e in ENGS}
        self.n = {}
        self.hw = {e: {} for e in ENGS}

    def emit(self, eng_name, build, waits=(), inc=None, k=1):
        need = []
        hw = self.hw[eng_name]
        for s, v in waits:
            if v > hw.get(s, 0):
                need.append((s, v))
                hw[s] = v
            self.n.setdefault(s, 0)
        if inc is not None:
            self.n[inc] = self.n.get(inc, 0) + k
        self.q[eng_name].append((tuple(need), build, inc, k))
        return self.n[inc] if inc is not None else None

    def barrier(self, eng_name, sems=None):
        sems = sems if sems is not None else list(self.n)
        waits = tuple((s, self.n[s]) for s in sems if self.n.get(s, 0) > 0)
        self.emit(eng_name, None, waits)

    def all_state(self):
        return dict(self.n)

    def flush(self, blk, mksem):
        semobjs = {k: mksem(k) for k in self.n}
        for e in ENGS:
            items = self.q[e]
            if not items:
                continue

            def run(eng, items=items):
                for need, build, inc, k in items:
                    for s, v in need:
                        eng.wait_ge(semobjs[s], v)
                    if build is None:
                        continue
                    ins = build(eng)
                    if inc is not None:
                        ins.then_inc(semobjs[inc], k)

            getattr(blk, e)(run)


def build_program(T=2048, debug=(), ncores=NCORE, stop_after=None,
                  y_sub=99, y_chunks=None):
    nch = T // 128
    TH = T // 2
    nc = bass.Bass("TRN2", target_bir_lowering=False)

    def din(name, shape, dt=F32):
        return nc.dram_tensor(name, shape, dt, kind="ExternalInput").ap()

    t_x = din("x", [T, D])
    t_xh = din("x_halo", [3, D])
    t_sel = din("sel", [128, ncores])
    t_winT = din("winT", [D, PROJ])
    t_woutT = din("woutT", [D, D])
    t_wgT = din("wgT", [D, HID])
    t_wuT = din("wuT", [D, HID])
    t_wdT = din("wdT", [HID, D])
    t_zb = din("zbias_pm", [128, 16])
    t_cw = din("convw_pm", [128, 48 * 4])
    t_cb = din("convb_pm", [128, 48])
    t_rw = din("rmsw_pm", [128, 16])
    t_dv = din("dvec", [NH])
    t_bo = din("blockones", [NH, D], BF16)
    t_ones = din("ones128b", [128, 1], BF16)
    t_e128b = din("eye128b", [128, 128], BF16)
    t_e128f = din("eye128f", [128, 128])
    t_e32f = din("eye32f", [32, 32])
    t_e3b = din("eye3b", [3, 3], BF16)
    t_nbig = din("negbig", [128, 4096])

    t_out = nc.dram_tensor("out", [T, D], F32, kind="ExternalOutput").ap()

    def scr(name, shape, dt):
        kind = "ExternalOutput" if name in debug else "Internal"
        return nc.dram_tensor(name, shape, dt, kind=kind).ap()

    d_xn = scr("d_xn", [T, D], BF16)
    d_xbc = scr("d_xbc", [CONVD, T], BF16)
    d_sz = scr("d_sz", [D, T], BF16)
    d_P = scr("d_P", [nch, 128, 16 * HD], BF16)
    d_acs = scr("d_acs", [nch, NH, 128], F32)
    d_cuma = scr("d_cuma", [nch, NH], F32)
    d_atot = scr("d_atot", [nch, NH], F32)
    d_srow = scr("d_srow", [2, T], F32)
    d_rstd = scr("d_rstd", [T], F32)
    d_g = scr("d_g", [D, T], BF16)
    d_yn = scr("d_yn", [D, T], BF16)
    d_t = scr("d_t", [D, T], BF16)
    d_xq = scr("d_xq", [16 * 128, T], BF16)
    d_wg = scr("d_wg", [128, 16 * 256], BF16)
    d_p = scr("d_p", [128, 2 * (T // 2)], BF16)
    d_f = scr("d_f", [16 * 128, T], BF16)
    cc_in = scr("cc_in", [128, 16 * HD], F32)
    cc_out = scr("cc_out", [128 * ncores, 16 * HD], F32)

    S = Sched()

    def dma(q, sem, out, in_, waits=(), **kw):
        return S.emit(q, (lambda eng, o=out, i=in_, kw=dict(kw): eng.dma_start(out=o, in_=i, **kw)),
                      waits, sem, 16)

    def op(e, build, waits=()):
        sem = {"vector": "cV", "scalar": "cA", "gpsimd": "cP", "tensor": "cT"}[e]
        return S.emit(e, build, waits, sem, 1)

    def nn(name):
        return S.n.get(name, 0)

    def barw(bar):
        return tuple((k, v) for k, v in bar.items() if v > 0)
    n = S.n
    for _k in ("cV", "cA", "cP", "cT"):
        n[_k] = 0

    with ExitStack() as top:
        E = top.enter_context
        blk = E(nc.Block())

        s_cw = E(nc.sbuf_tensor("s_cw", [128, 192], F32))
        s_cb = E(nc.sbuf_tensor("s_cb", [128, 48], F32))
        s_zb = E(nc.sbuf_tensor("s_zb", [128, 16], F32))
        s_rw = E(nc.sbuf_tensor("s_rw", [128, 16], F32))
        s_sel = E(nc.sbuf_tensor("s_sel", [128, ncores], F32))
        s_ones = E(nc.sbuf_tensor("s_ones", [128, 1], BF16))
        s_e128b = E(nc.sbuf_tensor("s_e128b", [128, 128], BF16))
        s_e128f = E(nc.sbuf_tensor("s_e128f", [128, 128], F32))
        s_e32f = E(nc.sbuf_tensor("s_e32f", [32, 32], F32))
        s_e3b = E(nc.sbuf_tensor("s_e3b", [3, 3], BF16))
        s_drep = E(nc.sbuf_tensor("s_drep", [128, NH], F32))
        s_flag = E(nc.sbuf_tensor("s_flag", [128, 2], F32))
        s_eps = E(nc.sbuf_tensor("s_eps", [128, 1], F32))
        s_var = E(nc.sbuf_tensor("s_var", [128, 16], F32))
        s_mxn = E(nc.sbuf_tensor("s_mxn", [128, 16], F32))
        s_acstm = E(nc.sbuf_tensor("s_acstm", [128, nch * 32], F32))
        ph123 = ExitStack()
        s_Ahm = ph123.enter_context(nc.sbuf_tensor("s_Ahm", [NH, T], F32))
        s_xhn = ph123.enter_context(nc.sbuf_tensor("s_xhn", [3, D], BF16))

        for dst, src in ((s_cw, t_cw), (s_cb, t_cb), (s_zb, t_zb), (s_rw, t_rw),
                         (s_sel, t_sel), (s_ones, t_ones), (s_e128b, t_e128b),
                         (s_e128f, t_e128f), (s_e32f, t_e32f), (s_e3b, t_e3b)):
            dma("sync", "dCst", dst[:, :], src)
        dma("scalar", "dCst2", s_drep[:, :], t_dv.unsqueeze(0).broadcast_to([128, NH]))
        consts_dS, consts_dA = nn("dCst"), nn("dCst2")
        for _e in ENGS:
            S.emit(_e, None, waits=(("dCst", consts_dS), ("dCst2", consts_dA)))

        op("vector", lambda eng: eng.memset(s_eps[:, :], 1e-5))
        eps_done = n["cV"]
        op("vector", lambda eng: eng.memset(s_flag[:, 0:1], 1.0))
        op("vector", lambda eng: eng.tensor_scalar(
            s_flag[:, 1:2], s_flag[:, 0:1], float(2.0**-30), -1.0, ALU.add, ALU.add),
            waits=(("cV", n["cV"]),))
        op("vector", lambda eng: eng.tensor_scalar(
            s_flag[:, 0:1], s_flag[:, 1:2], 0.0, None, ALU.is_gt),
            waits=(("cV", n["cV"]),))
        flag_done = n["cV"]

        with ExitStack() as ph:
            P = ph.enter_context
            s_x = [P(nc.sbuf_tensor(f"s_x{i}", [128, D], F32)) for i in range(2)]
            s_sq = P(nc.sbuf_tensor("s_sq", [128, D], F32))
            s_xnst = [P(nc.sbuf_tensor(f"s_xnst{i}", [128, D], BF16)) for i in range(2)]
            s_st = P(nc.sbuf_tensor("s_st", [128, 8], F32))
            s_xh = P(nc.sbuf_tensor("s_xh", [3, D], F32))
            s_hst = P(nc.sbuf_tensor("s_hst", [3, 8], F32))
            s_hsq = P(nc.sbuf_tensor("s_hsq", [3, D], F32))

            ld = {}
            done_v = {}
            for tb in range(min(2, T // 128)):
                ld[tb] = dma("sync", f"dLx{tb % 2}", s_x[tb % 2][:, :],
                             t_x[tb * 128:(tb + 1) * 128, :])
            st_dma = []
            for tb in range(T // 128):
                i = tb % 2
                xb = s_x[i]
                w = ((f"dLx{tb % 2}", ld[tb]),)
                op("vector", lambda eng, xb=xb: eng.reduce_sum(
                    s_st[:, 0:1], xb[:, :], AX.X), waits=w + (("cV", n["cV"]),))
                op("vector", lambda eng, xb=xb: eng.scalar_tensor_tensor(
                    s_sq[:, :], xb[:, :], 1.0, xb[:, :], ALU.mult, ALU.mult,
                    accum_out=s_st[:, 1:2]), waits=(("cV", n["cV"]),))
                op("vector", lambda eng: eng.tensor_scalar(
                    s_st[:, 2:3], s_st[:, 0:1], 1.0 / D, None, ALU.mult),
                    waits=(("cV", n["cV"]),))
                op("vector", lambda eng: eng.tensor_scalar(
                    s_st[:, 3:4], s_st[:, 1:2], 1.0 / D, None, ALU.mult),
                    waits=(("cV", n["cV"]),))
                op("vector", lambda eng: eng.scalar_tensor_tensor(
                    s_st[:, 4:5], s_st[:, 2:3], -1.0, s_st[:, 2:3], ALU.mult, ALU.mult),
                    waits=(("cV", n["cV"]),))
                op("vector", lambda eng, tb=tb: eng.scalar_tensor_tensor(
                    s_var[:, tb:tb + 1], s_st[:, 3:4], 1.0, s_st[:, 4:5], ALU.mult, ALU.add),
                    waits=(("cV", n["cV"]),))
                op("scalar", lambda eng, tb=tb: eng.activation(
                    s_st[:, 5:6], s_var[:, tb:tb + 1], AF.Sqrt, bias=s_eps[:, 0:1]),
                    waits=(("cV", n["cV"]), ("cV", eps_done)))
                op("vector", lambda eng: eng.reciprocal(
                    s_st[:, 6:7], s_st[:, 5:6]), waits=(("cA", n["cA"]), ("cV", n["cV"])))
                op("vector", lambda eng, xb=xb, i=i: eng.tensor_scalar(
                    s_xnst[i][:, :], xb[:, :], s_st[:, 2:3], s_st[:, 6:7],
                    ALU.subtract, ALU.mult),
                    waits=(("cV", n["cV"]), ("dXnSp", st_dma[-1] if st_dma else 0)))
                xn_ready = n["cV"]
                op("vector", lambda eng, tb=tb, i=i: eng.reduce_max(
                    s_mxn[:, tb:tb + 1], s_xnst[i][:, :], AX.X, apply_absolute_value=True),
                    waits=(("cV", xn_ready),))
                done_v[i] = n["cV"]
                st_dma.append(dma("sync", "dXnSp", d_xn[tb * 128:(tb + 1) * 128, :],
                                  s_xnst[i][:, :], waits=(("cV", xn_ready),)))
                if tb + 2 < T // 128:
                    ld[tb + 2] = dma("sync", f"dLx{i}", s_x[i][:, :],
                                     t_x[(tb + 2) * 128:(tb + 3) * 128, :],
                                     waits=(("cV", done_v[i]),))
            hld = dma("sync", "dHalo", s_xh[:, :], t_xh)
            op("vector", lambda eng: eng.reduce_sum(s_hst[:, 0:1], s_xh[:, :], AX.X),
               waits=(("dHalo", hld), ("cV", n["cV"])))
            op("vector", lambda eng: eng.scalar_tensor_tensor(
                s_hsq[:, :], s_xh[:, :], 1.0, s_xh[:, :], ALU.mult, ALU.mult,
                accum_out=s_hst[:, 1:2]), waits=(("cV", n["cV"]),))
            op("vector", lambda eng: eng.tensor_scalar(
                s_hst[:, 2:3], s_hst[:, 0:1], 1.0 / D, None, ALU.mult), waits=(("cV", n["cV"]),))
            op("vector", lambda eng: eng.tensor_scalar(
                s_hst[:, 3:4], s_hst[:, 1:2], 1.0 / D, None, ALU.mult), waits=(("cV", n["cV"]),))
            op("vector", lambda eng: eng.scalar_tensor_tensor(
                s_hst[:, 4:5], s_hst[:, 2:3], -1.0, s_hst[:, 2:3], ALU.mult, ALU.mult),
                waits=(("cV", n["cV"]),))
            op("vector", lambda eng: eng.scalar_tensor_tensor(
                s_hst[:, 5:6], s_hst[:, 3:4], 1.0, s_hst[:, 4:5], ALU.mult, ALU.add),
                waits=(("cV", n["cV"]),))
            op("scalar", lambda eng: eng.activation(
                s_hst[:, 6:7], s_hst[:, 5:6], AF.Sqrt, bias=s_eps[0:3, 0:1]),
                waits=(("cV", n["cV"]),))
            op("vector", lambda eng: eng.reciprocal(s_hst[:, 7:8], s_hst[:, 6:7]),
               waits=(("cA", n["cA"]), ("cV", n["cV"])))
            op("vector", lambda eng: eng.tensor_scalar(
                s_xhn[:, :], s_xh[:, :], s_hst[:, 2:3], s_hst[:, 7:8],
                ALU.subtract, ALU.mult), waits=(("cV", n["cV"]),))
            halo_v = n["cV"]
            ln_sp, ln_cV = nn("dXnSp"), n["cV"]

        with ExitStack() as ph:
            P = ph.enter_context
            EXT = T + XOFF
            s_xnfm = P(nc.sbuf_tensor("s_xnfm", [128, 16 * EXT], BF16))
            s_wb = [P(nc.sbuf_tensor(f"s_wb{i}", [128, 16 * 512], BF16)) for i in range(2)]
            s_raw = P(nc.sbuf_tensor("s_raw", [128, 4 * (T + 3)], BF16))
            s_acc = [P(nc.sbuf_tensor(f"s_acc{i}", [128, T], F32)) for i in range(2)]
            s_stg = [P(nc.sbuf_tensor(f"s_stg{i}", [128, T], BF16)) for i in range(4)]
            s_Atmp = P(nc.sbuf_tensor("s_At", [NH, T], F32))
            pacc = [P(nc.psum_tensor(f"pacc{i}", [128, 512], F32)) for i in range(4)]
            ph3 = P(nc.psum_tensor("ph3", [128, 16], BF16))

            ln_bar = barw(S.all_state())
            xnfm_ld = []
            for ks in range(16):
                xnfm_ld.append(dma(
                    "scalar", "dXnRd", s_xnfm[:, ks * EXT + XOFF: ks * EXT + XOFF + T],
                    d_xn[:, ks * 128:(ks + 1) * 128], waits=ln_bar,
                    transpose=True))
            ph3_ev = 0
            for ks in range(16):
                tt = op("tensor", lambda eng, ks=ks: eng.transpose(
                    ph3[:, 0:3], s_xhn[0:3, ks * 128:(ks + 1) * 128], s_e3b[:, :]),
                    waits=ln_bar + (("cV", max(halo_v, ph3_ev)), ("cT", n["cT"])))
                ph3_ev = op("vector", lambda eng, ks=ks: eng.tensor_copy(
                    s_xnfm[:, ks * EXT + XOFF - 3: ks * EXT + XOFF], ph3[:, 0:3]),
                    waits=(("cT", tt), ("cV", n["cV"])))
            xnext_dA = nn("dXnRd")
            xnext_cV = n["cV"]

            groups = [(g * 512, 512) for g in range(16)] + [(8192, 32)]
            wload = {}

            def load_group(gi, waits=()):
                mlo, msz = groups[gi]
                sl = gi % 2
                srcw = t_winT.rearrange("(ks kp) m -> kp ks m", kp=128)[:, :, mlo:mlo + msz]
                wload[gi] = dma("gpsimd", f"dWin{sl}", s_wb[sl][:, 0:16 * msz].rearrange(
                    "p (ks m) -> p ks m", ks=16), srcw, waits=waits)

            load_group(0, waits=ln_bar)
            load_group(1, waits=ln_bar)

            raw_free = [0] * 4
            raw_free_sem = ["cV"] * 8
            prev_acc_user = {}
            stg_user = {}
            A_evac = []

            for gi, (mlo, msz) in enumerate(groups):
                sl = gi % 2
                wv = ((f"dWin{gi % 2}", wload[gi]),)
                is_xbc = gi < 12
                is_z = gi in (12, 13, 14, 15)
                is_A = not (is_xbc or is_z)
                nmb = msz // 128 if msz >= 128 else 1
                for mb in range(nmb):
                    sp = spans(0, T + 3, 512) if is_xbc else spans(0, T, 512)
                    for si, (j0, j1) in enumerate(sp):
                        pslot = (mb * len(sp) + si) % 4
                        pt = pacc[pslot]
                        nw = j1 - j0
                        np_ = 32 if is_A else 128
                        pw = prev_acc_user.get(pslot, ("cV", 0))
                        for ks in range(16):
                            if is_xbc:
                                rhs = s_xnfm[:, ks * EXT + XOFF - 3 + j0: ks * EXT + XOFF - 3 + j1]
                            else:
                                rhs = s_xnfm[:, ks * EXT + XOFF + j0: ks * EXT + XOFF + j1]
                            lhsT = s_wb[sl][:, 0:16 * msz].rearrange(
                                "p (ks m) -> p ks m", ks=16)[:, ks,
                                mb * 128:mb * 128 + (128 if msz >= 128 else msz)]
                            wts = (wv + (pw,)) if ks == 0 else ()
                            op("tensor", lambda eng, pt=pt, lhsT=lhsT, rhs=rhs, ks=ks,
                               nw=nw, np_=np_:
                               eng.matmul(pt[0:np_, 0:nw], lhsT, rhs,
                                          start=(ks == 0), stop=(ks == 15)),
                               waits=wts + (("dXnRd", xnext_dA), ("cV", xnext_cV)))
                        mm_done = n["cT"]
                        if is_xbc:
                            dst = s_raw[:, mb * (T + 3) + j0: mb * (T + 3) + j1]
                            ev = op("vector", lambda eng, dst=dst, pt=pt, nw=nw:
                                    eng.tensor_copy(dst, pt[:, 0:nw]),
                                    waits=(("cT", mm_done),
                                           (raw_free_sem[mb], raw_free[mb]),
                                           ("cV", n["cV"])))
                            prev_acc_user[pslot] = ("cV", ev)
                        elif is_z:
                            zb = (gi - 12) * 4 + mb
                            stg = s_stg[mb % 2]
                            su = stg_user.get(mb % 2, ("cV", 0))
                            ev = op("scalar", lambda eng, stg=stg, pt=pt, j0=j0, nw=nw, zb=zb:
                                    eng.activation(stg[:, j0:j0 + nw], pt[:, 0:nw], AF.Silu,
                                                   bias=s_zb[:, zb:zb + 1]),
                                    waits=(("cT", mm_done), su, ("cA", n["cA"])))
                            prev_acc_user[pslot] = ("cA", ev)
                            if si == len(sp) - 1:
                                sd = dma("sync", f"dSz{mb % 2}",
                                         d_sz[zb * 128:(zb + 1) * 128, :], stg[:, :],
                                         waits=(("cA", ev),))
                                stg_user[mb % 2] = (f"dSz{mb % 2}", sd)
                        else:
                            ev = op("vector", lambda eng, pt=pt, j0=j0, nw=nw:
                                    eng.tensor_copy(s_Ahm[0:NH, j0:j0 + nw], pt[0:NH, 0:nw]),
                                    waits=(("cT", mm_done), ("cV", n["cV"])))
                            prev_acc_user[pslot] = ("cV", ev)
                            A_evac.append(ev)
                if gi + 2 < len(groups):
                    load_group(gi + 2, waits=(("cT", n["cT"]),))
                if is_xbc:
                    for fb in range(4):
                        gfb = gi * 4 + fb
                        e = "vector"
                        csem = "cV"
                        a0, a1 = s_acc[0], s_acc[1]
                        w0 = (("cV", n["cV"]), (csem, n[csem]), ("cA", n["cA"]))
                        op(e, lambda eng, fb=fb, gfb=gfb, a0=a0: eng.tensor_scalar(
                            a0[:, :], s_raw[:, fb * (T + 3): fb * (T + 3) + T],
                            s_cw[:, gfb * 4: gfb * 4 + 1], s_cb[:, gfb:gfb + 1],
                            ALU.mult, ALU.add), waits=w0)
                        cur, nxt = a0, a1
                        for w in (1, 2, 3):
                            op(e, lambda eng, fb=fb, gfb=gfb, w=w, cur=cur, nxt=nxt:
                               eng.scalar_tensor_tensor(
                                   nxt[:, :], s_raw[:, fb * (T + 3) + w: fb * (T + 3) + w + T],
                                   s_cw[:, gfb * 4 + w: gfb * 4 + w + 1], cur[:, :],
                                   ALU.mult, ALU.add),
                               waits=((csem, n[csem]),))
                            cur, nxt = nxt, cur
                        conv_done = n[csem]
                        raw_free[fb] = conv_done
                        raw_free_sem[fb] = csem
                        stg = s_stg[2 + fb % 2]
                        su = stg_user.get(2 + fb % 2, ("cV", 0))
                        sv = op("scalar", lambda eng, cur=cur, stg=stg: eng.activation(
                            stg[:, :], cur[:, :], AF.Silu),
                            waits=((csem, conv_done), su, ("cA", n["cA"])))
                        sd = dma("sync", f"dXbc{fb % 2}",
                                 d_xbc[gfb * 128:(gfb + 1) * 128, :], stg[:, :],
                                 waits=(("cA", sv),))
                        stg_user[2 + fb % 2] = (f"dXbc{fb % 2}", sd)
            spills_done = (("dXbc0", nn("dXbc0")), ("dXbc1", nn("dXbc1")),
                           ("dSz0", nn("dSz0")), ("dSz1", nn("dSz1")))

            aw = (("cV", max(A_evac)),)
            op("scalar", lambda eng: eng.activation(s_Atmp[:, :], s_Ahm[:, :], AF.Abs),
               waits=aw + (("cA", n["cA"]),))
            op("scalar", lambda eng: eng.activation(s_Atmp[:, :], s_Atmp[:, :],
                                                    AF.Exp, scale=-1.0), waits=(("cA", n["cA"]),))
            op("scalar", lambda eng: eng.activation(s_Atmp[:, :], s_Atmp[:, :],
                                                    AF.Ln, bias=1.0), waits=(("cA", n["cA"]),))
            sp_done = n["cA"]
            op("vector", lambda eng: eng.tensor_relu(s_Ahm[:, :], s_Ahm[:, :]),
               waits=(("cA", sp_done), ("cV", n["cV"])))
            op("vector", lambda eng: eng.tensor_add(s_Ahm[:, :], s_Ahm[:, :], s_Atmp[:, :]),
               waits=(("cV", n["cV"]),))
            op("vector", lambda eng: eng.tensor_scalar(
                s_Ahm[:, :], s_Ahm[:, :], -1.0, None, ALU.mult), waits=(("cV", n["cV"]),))
            ahm_done = n["cV"]

        if stop_after == "inproj":
            S.barrier("sync"); S.barrier("gpsimd"); S.barrier("scalar")
            S.flush(blk, lambda name: E(nc.semaphore(name)))
            return nc
        with ExitStack() as ph:
            P = ph.enter_context
            s_S = P(nc.sbuf_tensor("s_S", [128, 16 * HD], F32))
            s_Stmp = P(nc.sbuf_tensor("s_Stmp", [128, 16 * HD], F32))
            s_acs = P(nc.sbuf_tensor("s_acs", [NH, 128], F32))
            s_ds1 = P(nc.sbuf_tensor("s_ds1", [NH, 128], F32))
            s_dstm = P(nc.sbuf_tensor("s_dstm", [128, 32], F32))
            s_eahr = P(nc.sbuf_tensor("s_eahr", [128, 16], F32))
            s_eah = P(nc.sbuf_tensor("s_eah", [128, 16], F32))
            s_cum = P(nc.sbuf_tensor("s_cum", [NH, 1], F32))
            s_btm = [P(nc.sbuf_tensor(f"s_btm{i}", [128, D], BF16)) for i in range(2)]
            s_xtm = [P(nc.sbuf_tensor(f"s_xtm{i}", [128, D], BF16)) for i in range(2)]
            s_bd = P(nc.sbuf_tensor("s_bd", [128, D], BF16))
            p_st = P(nc.psum_tensor("p_st", [128, 16 * HD], F32))
            p_tp = P(nc.psum_tensor("p_tp", [128, 32], F32))

            op("vector", lambda eng: eng.memset(s_S[:, :], 0.0),
               waits=(("cV", n["cV"]),))
            op("vector", lambda eng: eng.memset(s_cum[:, :], 0.0),
               waits=(("cV", n["cV"]),))

            btm_ld, xtm_ld = {}, {}
            btm_user, xtm_user = {}, {}
            eah_user = ("cA", 0)
            for c in range(min(2, nch)):
                i = c % 2
                btm_ld[c] = dma("scalar", f"dBtm{i}", s_btm[i][:, :],
                                d_xbc[D:2 * D, c * 128:(c + 1) * 128],
                                waits=spills_done, transpose=True)
                xtm_ld[c] = dma("scalar", f"dXtm{i}", s_xtm[i][:, :],
                                d_xbc[0:D, c * 128:(c + 1) * 128],
                                waits=spills_done, transpose=True)

            for c in range(nch):
                i = c % 2
                sc = op("vector", lambda eng, c=c: eng.tensor_tensor_scan(
                    s_acs[:, :], s_Ahm[:, c * 128:(c + 1) * 128],
                    s_Ahm[:, c * 128:(c + 1) * 128], 0.0, ALU.add, ALU.bypass),
                    waits=(("cV", max(ahm_done, n["cV"])), ("dAcs", nn("dAcs")),
                           ("dAt", nn("dAt")), ("cT", n["cT"]), ("cA", n["cA"])))
                acs_dma = dma("sync", "dAcs", d_acs[c, :, :], s_acs[:, :],
                              waits=(("cV", sc),))
                at_dma = dma("sync", "dAt", d_atot[c, :].unsqueeze(1),
                             s_acs[:, 127:128], waits=(("cV", sc),))
                cum_dma = dma("sync", "dCum", d_cuma[c, :].unsqueeze(1), s_cum[:, :],
                              waits=(("cV", n["cV"]),))
                op("vector", lambda eng: eng.tensor_add(
                    s_cum[:, :], s_cum[:, :], s_acs[:, 127:128]),
                    waits=(("dCum", cum_dma), ("cV", n["cV"])))
                tt = op("tensor", lambda eng: eng.transpose(
                    p_tp[:, 0:32], s_acs[:, :], s_e32f[:, :]),
                    waits=(("cV", sc), ("cT", n["cT"])))
                op("vector", lambda eng, c=c: eng.tensor_copy(
                    s_acstm[:, c * 32:(c + 1) * 32], p_tp[:, 0:32]),
                    waits=(("cT", tt), ("cV", n["cV"])))
                acstm_ev = n["cV"]
                op("vector", lambda eng: eng.tensor_scalar(
                    s_ds1[:, :], s_acs[:, :], s_acs[:, 127:128], None, ALU.subtract),
                    waits=(("cV", n["cV"]), ("dAcs", acs_dma), ("cT", n["cT"])))
                dsv = op("scalar", lambda eng: eng.activation(
                    s_ds1[:, :], s_ds1[:, :], AF.Exp, scale=-1.0),
                    waits=(("cV", n["cV"]), ("cA", n["cA"])))
                tt2 = op("tensor", lambda eng: eng.transpose(
                    p_tp[:, 0:32], s_ds1[:, :], s_e32f[:, :]),
                    waits=(("cA", dsv), ("cV", acstm_ev), ("cT", n["cT"])))
                dst_ev = op("vector", lambda eng: eng.tensor_copy(
                    s_dstm[:, :], p_tp[:, 0:32]), waits=(("cT", tt2), ("cV", n["cV"])))
                bdv = op("vector", lambda eng, i=i: eng.tensor_tensor(
                    s_bd[:, :].rearrange("p (h x) -> p h x", h=NH),
                    s_btm[i][:, :].rearrange("p (h x) -> p h x", h=NH),
                    s_dstm[:, :].unsqueeze(2).broadcast_to([128, NH, HD]), ALU.mult),
                    waits=(("cV", dst_ev), (f"dBtm{i}", btm_ld[c]),
                           ("cV", btm_user.get(i, 0))))
                stw = (("cV", bdv), (f"dXtm{i}", xtm_ld[c]), ("cV", n["cV"]))
                for h2 in range(NH):
                    pb2, hf2 = h2 // 2, h2 % 2
                    op("tensor", lambda eng, h2=h2, i=i, pb2=pb2, hf2=hf2: eng.matmul(
                        p_st[hf2 * 64:hf2 * 64 + 64, pb2 * HD:(pb2 + 1) * HD],
                        s_bd[:, h2 * HD:(h2 + 1) * HD],
                        s_xtm[i][:, h2 * HD:(h2 + 1) * HD], start=True, stop=True),
                        waits=stw if h2 == 0 else (("cT", n["cT"]),))
                st_done = n["cT"]
                btm_user[i] = bdv
                if c + 2 < nch:
                    btm_ld[c + 2] = dma("scalar", f"dBtm{i}", s_btm[i][:, :],
                                        d_xbc[D:2 * D, (c + 2) * 128:(c + 3) * 128],
                                        waits=(("cV", bdv),), transpose=True)
                    xtm_ld[c + 2] = dma("scalar", f"dXtm{i}", s_xtm[i][:, :],
                                        d_xbc[0:D, (c + 2) * 128:(c + 3) * 128],
                                        waits=(("cT", st_done),), transpose=True)
                pd = dma("gpsimd", "dP", d_P[c, :, :], s_S[:, :],
                         waits=(("cV", n["cV"]),))
                dma("scalar", "dEah", s_eahr[0:64, :],
                    d_atot[c, :].rearrange("(pb hf) -> hf pb", hf=2)[0:1, :]
                    .broadcast_to([64, 16]),
                    waits=(("dAt", at_dma), eah_user), allow_slow_non_contiguous=True)
                eb = dma("scalar", "dEah", s_eahr[64:128, :],
                         d_atot[c, :].rearrange("(pb hf) -> hf pb", hf=2)[1:2, :]
                         .broadcast_to([64, 16]),
                         waits=(("dAt", at_dma), eah_user), allow_slow_non_contiguous=True)
                ev = op("scalar", lambda eng: eng.activation(
                    s_eah[:, :], s_eahr[:, :], AF.Exp),
                    waits=(("dEah", eb), ("cA", n["cA"])))
                eah_user = ("cA", ev)
                u1 = op("vector", lambda eng: eng.tensor_tensor(
                    s_Stmp[:, :].rearrange("p (h x) -> p h x", h=16),
                    s_S[:, :].rearrange("p (h x) -> p h x", h=16),
                    s_eah[:, :].unsqueeze(2).broadcast_to([128, 16, HD]), ALU.mult),
                    waits=(("cA", ev), ("dP", pd), ("cV", n["cV"])))
                op("vector", lambda eng: eng.tensor_add(
                    s_S[:, :], s_Stmp[:, :], p_st[:, :]),
                    waits=(("cV", u1), ("cT", st_done)))
            send_dma = dma("sync", "dCc", cc_in, s_S[:, :], waits=(("cV", n["cV"]),))
            cc_tok = S.emit("gpsimd", lambda eng: eng.collective_compute(
                "AllGather", ALU.bypass, replica_groups=[list(range(ncores))],
                ins=[cc_in], outs=[cc_out]), waits=(("dCc", send_dma),), inc="dC", k=1)
            states_end = S.all_state()
        ph123.close()

        if stop_after == "states":
            S.barrier("sync"); S.barrier("gpsimd"); S.barrier("scalar")
            S.flush(blk, lambda name: E(nc.semaphore(name)))
            return nc
        with ExitStack() as ph:
            P = ph.enter_context
            s_xq = P(nc.sbuf_tensor("s_xq", [128, 16 * T], BF16))

            bar = states_end
            w0 = (("cV", bar["cV"]), ("cA", bar["cA"]), ("cT", bar["cT"]))
            qs = ExitStack()
            Q = qs.enter_context
            s_srow = Q(nc.sbuf_tensor("s_srow", [128, T], BF16))
            s_inv = Q(nc.sbuf_tensor("s_inv", [128, T], BF16))
            s_q = Q(nc.sbuf_tensor("s_q", [128, 8], F32))
            s_pk = Q(nc.sbuf_tensor("s_pk", [128, 32], F32))
            p_tp2 = Q(nc.psum_tensor("p_tp2", [32, 128], F32))
            s_rowev = Q(nc.sbuf_tensor("s_rowev", [32, 128], F32))
            ntb = T // 128
            op("vector", lambda eng: eng.memset(s_pk[:, :], 0.0),
               waits=(("cV", n["cV"]),))
            for tb in range(ntb):
                op("vector", lambda eng, tb=tb: eng.tensor_scalar(
                    s_q[:, 0:1], s_var[:, tb:tb + 1], 1e-5, None, ALU.add),
                    waits=w0 + (("cV", n["cV"]),))
                op("vector", lambda eng: eng.reciprocal(s_q[:, 1:2], s_q[:, 0:1]),
                   waits=(("cV", n["cV"]),))
                op("vector", lambda eng, tb=tb: eng.tensor_tensor(
                    s_q[:, 2:3], s_var[:, tb:tb + 1], s_q[:, 1:2], ALU.mult),
                    waits=(("cV", n["cV"]),))
                op("vector", lambda eng: eng.tensor_scalar(
                    s_q[:, 3:4], s_q[:, 2:3], 1e-8, None, ALU.add), waits=(("cV", n["cV"]),))
                op("scalar", lambda eng: eng.activation(
                    s_q[:, 4:5], s_q[:, 3:4], AF.Sqrt), waits=(("cV", n["cV"]), ("cA", n["cA"])))
                op("vector", lambda eng: eng.reciprocal(s_q[:, 5:6], s_q[:, 4:5]),
                   waits=(("cA", n["cA"]), ("cV", n["cV"])))
                op("vector", lambda eng, tb=tb: eng.tensor_tensor(
                    s_q[:, 6:7], s_q[:, 5:6], s_mxn[:, tb:tb + 1], ALU.mult),
                    waits=(("cV", n["cV"]),))
                op("vector", lambda eng, tb=tb: eng.reciprocal(
                    s_q[:, 7:8], s_mxn[:, tb:tb + 1]), waits=(("cV", n["cV"]),))
                op("vector", lambda eng, tb=tb: eng.tensor_scalar(
                    s_pk[:, tb:tb + 1], s_q[:, 7:8], 127.0, None, ALU.mult),
                    waits=(("cV", n["cV"]),))
                op("vector", lambda eng, tb=tb: eng.tensor_scalar(
                    s_pk[:, 16 + tb:17 + tb], s_q[:, 6:7], 1.0 / 127.0, None, ALU.mult),
                    waits=(("cV", n["cV"]),))
            pk_done = n["cV"]
            tt = op("tensor", lambda eng: eng.transpose(
                p_tp2[:, :], s_pk[:, :], s_e128f[:, :]),
                waits=(("cV", pk_done), ("cT", bar["cT"])))
            rv = op("vector", lambda eng: eng.tensor_copy(s_rowev[:, :], p_tp2[:, :]),
                    waits=(("cT", tt), ("cV", n["cV"])))
            r1 = dma("sync", "dSr", d_srow[0:1, :].rearrange("one (tb p) -> (one tb) p", p=128),
                     s_rowev[0:ntb, :], waits=(("cV", rv),))
            r2 = dma("sync", "dSr", d_srow[1:2, :].rearrange("one (tb p) -> (one tb) p", p=128),
                     s_rowev[16:16 + ntb, :], waits=(("cV", rv),))
            b1 = dma("gpsimd", "dSb", s_srow[:, :],
                     d_srow[0:1, :].broadcast_to([128, T]), waits=(("dSr", r2),))
            b2 = dma("gpsimd", "dSb", s_inv[:, :],
                     d_srow[1:2, :].broadcast_to([128, T]), waits=(("dSr", r2),))

            if True:
                s_xsl = [Q(nc.sbuf_tensor(f"s_xsl{i}", [128, T], BF16)) for i in range(2)]
                s_y = [Q(nc.sbuf_tensor(f"s_y{i}", [128, T], F32)) for i in range(2)]
                s_t2 = [Q(nc.sbuf_tensor(f"s_t2{i}", [128, T], F32)) for i in range(2)]
                s_t1 = [Q(nc.sbuf_tensor(f"s_t1{i}", [128, T], F32)) for i in range(2)]
                xsl_user = {}
                for kb in range(16):
                    e = "vector"
                    cs = "cV"
                    i = kb % 2
                    xl = dma("scalar", f"dXsl{i}", s_xsl[i][:, :],
                             d_xn[:, kb * 128:(kb + 1) * 128],
                             waits=(("dXnSp", ln_sp), ("dSb", max(b1, b2)),
                                    xsl_user.get(i, ("cV", 0))),
                             transpose=True)
                    op(e, lambda eng, i=i: eng.tensor_tensor(
                        s_y[i][:, :], s_xsl[i][:, :], s_srow[:, :], ALU.mult),
                        waits=((f"dXsl{i}", xl), ("cV", flag_done), (cs, n[cs])))
                    op(e, lambda eng, i=i: eng.tensor_scalar(
                        s_t2[i][:, :], s_y[i][:, :], C23, -C23, ALU.add, ALU.add),
                        waits=((cs, n[cs]),))
                    op(e, lambda eng, i=i: eng.tensor_scalar(
                        s_t1[i][:, :], s_y[i][:, :], C52, -C52, ALU.add, ALU.add),
                        waits=((cs, n[cs]),))
                    op(e, lambda eng, i=i: eng.tensor_sub(
                        s_t1[i][:, :], s_t1[i][:, :], s_t2[i][:, :]),
                        waits=((cs, n[cs]),))
                    op(e, lambda eng, i=i: eng.scalar_tensor_tensor(
                        s_t2[i][:, :], s_t1[i][:, :], s_flag[:, 0:1], s_t2[i][:, :],
                        ALU.mult, ALU.add), waits=((cs, n[cs]),))
                    fin = op(e, lambda eng, i=i, kb=kb: eng.scalar_tensor_tensor(
                        s_xq[:, kb * T:(kb + 1) * T], s_t2[i][:, :], 127.0, s_inv[:, :],
                        ALU.min, ALU.mult), waits=((cs, n[cs]),))
                    xsl_user[i] = (cs, fin)
                xq_cV, xq_cP = n["cV"], n["cP"]
                if "d_xq" in debug:
                    for kb in range(16):
                        dma("sync", "dDbgQ", d_xq[kb * 128:(kb + 1) * 128, :],
                            s_xq[:, kb * T:(kb + 1) * T],
                            waits=(("cV", xq_cV), ("cP", xq_cP)))
            qs.close()

            s_wg = P(nc.sbuf_tensor("s_wg", [128, 16 * 256], BF16))
            s_wu = P(nc.sbuf_tensor("s_wu", [128, 16 * 256], BF16))
            s_wd = P(nc.sbuf_tensor("s_wd", [128, 4 * 2048], BF16))
            s_fa = P(nc.sbuf_tensor("s_fa", [128, 16 * TH], BF16))
            s_psb = [P(nc.sbuf_tensor(f"s_psb{i}", [128, 2 * TH], BF16)) for i in range(2)]
            s_part = P(nc.sbuf_tensor("s_part", [128, 16 * 1024], BF16))
            s_gs = P(nc.sbuf_tensor("s_gs", [128, 512], BF16))
            s_fst = [P(nc.sbuf_tensor(f"s_fst{i}", [128, TH], BF16)) for i in range(2)]
            p_up = [P(nc.psum_tensor(f"p_up{i}", [128, 512], F32)) for i in range(4)]
            p_dn = [P(nc.psum_tensor(f"p_dn{i}", [128, 512], F32)) for i in range(4)]

            wgv = wuv = wdv = ("cA", 0)
            pup_user, puu_user, pdn_user = {}, {}, {}
            wg_user = wu_user = ("cT", 0)
            wd_user = {0: ("cT", 0), 1: ("cT", 0)}
            part_user = {}
            psb_user = {0: ("cT", 0), 1: ("cT", 0)}
            fa_user = ("cA", 0)
            gs_user = {}
            for hh in range(2):
                toff = hh * TH
                tq_sp = spans(0, TH, 512)
                op("vector", lambda eng: eng.memset(s_fa[:, :], 0.0),
                   waits=(("cV", n["cV"]), fa_user, ("dDbgF", nn("dDbgF")),
                          ("dAc0", nn("dAc0")), ("dAc1", nn("dAc1"))))
                fa_zero = n["cV"]
                for g in range(32):
                    sl = g % 2
                    lg = dma("gpsimd", "dWg", s_wg[:, :].rearrange("p (ks m) -> p ks m", ks=16),
                             t_wgT.rearrange("(ks kp) m -> kp ks m", kp=128)[:, :, g * 256:(g + 1) * 256],
                             waits=(wg_user, ("cV", xq_cV), ("cP", xq_cP),
                                    ("dDbgW", nn("dDbgW"))))
                    lu = dma("gpsimd", "dWu", s_wu[:, :].rearrange("p (ks m) -> p ks m", ks=16),
                             t_wuT.rearrange("(ks kp) m -> kp ks m", kp=128)[:, :, g * 256:(g + 1) * 256],
                             waits=(wu_user, ("cV", xq_cV), ("cP", xq_cP)))
                    ld_ = dma("gpsimd", f"dWd{sl}", s_wd[:, sl * 2 * 2048:(sl + 1) * 2 * 2048].rearrange(
                        "p (ks m) -> p ks m", ks=2),
                        t_wdT.rearrange("(ks kp) m -> kp ks m", kp=128)[:, g * 2:(g + 1) * 2, :],
                        waits=(wd_user[sl], ("cV", xq_cV), ("cP", xq_cP)))
                    wgv = ("cA", op("scalar", lambda eng: eng.activation(
                        s_wg[:, :], s_wg[:, :], AF.Sign), waits=(("dWg", lg), ("cA", n["cA"]))))
                    wuv = ("cA", op("scalar", lambda eng: eng.activation(
                        s_wu[:, :], s_wu[:, :], AF.Sign), waits=(("dWu", lu), ("cA", n["cA"]))))
                    wdv = ("cA", op("scalar", lambda eng, sl=sl: eng.activation(
                        s_wd[:, sl * 2 * 2048:(sl + 1) * 2 * 2048],
                        s_wd[:, sl * 2 * 2048:(sl + 1) * 2 * 2048], AF.Sign),
                        waits=((f"dWd{sl}", ld_), ("cA", n["cA"]))))
                    for mb in range(2):
                        for ti, (q0, q1) in enumerate(tq_sp):
                            nw = q1 - q0
                            idx = (mb * len(tq_sp) + ti) % 2
                            pg = p_up[idx]
                            pu = p_up[2 + idx]
                            for ks in range(16):
                                wts = (wgv, ("cV", xq_cV), ("cP", xq_cP),
                                       ("cA", n["cA"]),
                                       pup_user.get(idx, ("cA", 0))) if ks == 0 else ()
                                op("tensor", lambda eng, pg=pg, ks=ks, mb=mb, q0=q0, nw=nw, toff=toff:
                                   eng.matmul(pg[:, 0:nw],
                                              s_wg[:, :].rearrange("p (ks m) -> p ks m", ks=16)[:, ks, mb * 128:(mb + 1) * 128],
                                              s_xq[:, ks * T + toff + q0: ks * T + toff + q0 + nw],
                                              start=(ks == 0), stop=(ks == 15)), waits=wts)
                            gmm = n["cT"]
                            gv = op("scalar", lambda eng, pg=pg, nw=nw: eng.activation(
                                s_gs[:, 0:nw], pg[:, 0:nw], AF.Silu),
                                waits=(("cT", gmm), ("cA", n["cA"]), ("cV", n["cV"])))
                            pup_user[idx] = ("cA", gv)
                            for ks in range(16):
                                wts = (wuv, ("cA", gv),
                                       puu_user.get(idx, ("cV", 0))) if ks == 0 else ()
                                op("tensor", lambda eng, pu=pu, ks=ks, mb=mb, q0=q0, nw=nw, toff=toff:
                                   eng.matmul(pu[:, 0:nw],
                                              s_wu[:, :].rearrange("p (ks m) -> p ks m", ks=16)[:, ks, mb * 128:(mb + 1) * 128],
                                              s_xq[:, ks * T + toff + q0: ks * T + toff + q0 + nw],
                                              start=(ks == 0), stop=(ks == 15)), waits=wts)
                            umm = n["cT"]
                            pv = op("vector", lambda eng, pu=pu, sl=sl, mb=mb, q0=q0, nw=nw:
                                    eng.tensor_tensor(
                                        s_psb[sl][:, mb * TH + q0: mb * TH + q0 + nw],
                                        pu[:, 0:nw], s_gs[:, 0:nw], ALU.mult),
                                    waits=(("cT", umm), ("cA", gv), psb_user[sl],
                                           ("cV", n["cV"])))
                            puu_user[idx] = ("cV", pv)
                    psb_done = n["cV"]
                    if "d_p" in debug and hh == 0 and g == 31:
                        dma("sync", "dDbgP", d_p, s_psb[sl][:, :],
                            waits=(("cV", psb_done),))
                    if "d_wg" in debug and hh == 0 and g == 31:
                        dma("sync", "dDbgW", d_wg, s_wg[:, :], waits=(("cV", psb_done),))
                    wg_user = ("cT", n["cT"])
                    wu_user = ("cT", n["cT"])
                    if g % 2 == 1:
                        for ti, (q0, q1) in enumerate(tq_sp):
                            nw = q1 - q0
                            for db in range(16):
                                pd_ = p_dn[db % 4]
                                for kk in range(4):
                                    wts = (wdv, ("cV", psb_done),
                                           part_user.get(ti, (f"dAc{ti}", 0)),
                                           pdn_user.get(db % 4, ("cV", 0))) if kk == 0 else ()
                                    op("tensor", lambda eng, pd_=pd_, kk=kk, db=db, q0=q0, nw=nw:
                                       eng.matmul(pd_[:, 0:nw],
                                                  s_wd[:, :].rearrange("p (ks m) -> p ks m", ks=4)[:, kk, db * 128:(db + 1) * 128],
                                                  s_psb[kk // 2][:, (kk % 2) * TH + q0: (kk % 2) * TH + q0 + nw],
                                                  start=(kk == 0), stop=(kk == 3)), waits=wts + ((("dDbgP", nn("dDbgP")),) if kk == 0 else ()))
                                dmm = n["cT"]
                                pdn_user[db % 4] = ("cV", op(
                                    "vector", lambda eng, pd_=pd_, db=db, nw=nw, ti=ti:
                                    eng.tensor_copy(
                                        s_part[:, db * 1024 + ti * 512: db * 1024 + ti * 512 + nw],
                                        pd_[:, 0:nw]),
                                    waits=(("cT", dmm), ("cV", n["cV"]))))
                            evd = n["cV"]
                            ac = dma("gpsimd", f"dAc{ti}", s_fa[:, :].rearrange(
                                "p (db t) -> p db t", db=16)[:, :, q0:q1],
                                s_part[:, :].rearrange("p (db t) -> p db t", db=16)[:, :, ti * 512:ti * 512 + nw],
                                waits=(("cV", evd), ("cV", fa_zero)), accum_op=ALU.add)
                            part_user[ti] = (f"dAc{ti}", ac)
                        wd_user[0] = ("cT", n["cT"])
                        wd_user[1] = ("cT", n["cT"])
                        psb_user[0] = ("cT", n["cT"])
                        psb_user[1] = ("cT", n["cT"])
                accw = tuple(part_user.get(ti, (f"dAc{ti}", 0)) for ti in range(len(tq_sp)))
                if "d_f" in debug:
                    for db in range(16):
                        dma("sync", "dDbgF", d_f[db * 128:(db + 1) * 128, toff:toff + TH],
                            s_fa[:, db * TH:(db + 1) * TH], waits=accw)
                for db in range(16):
                    i = db % 2
                    su = gs_user.get(i)
                    sv = op("scalar", lambda eng, db=db, i=i: eng.activation(
                        s_fst[i][:, :], s_fa[:, db * TH:(db + 1) * TH], AF.Silu),
                        waits=accw + (("cA", n["cA"]),) + ((su,) if su else ()))
                    gs_user[i] = (f"dGs{i}", dma(
                        "sync", f"dGs{i}", d_g[db * 128:(db + 1) * 128, toff:toff + TH],
                        s_fst[i][:, :], waits=(("cA", sv),)))
                fa_user = ("cA", n["cA"])
            ffn_end = S.all_state()

        if stop_after == "ffn":
            S.barrier("sync"); S.barrier("gpsimd"); S.barrier("scalar")
            S.flush(blk, lambda name: E(nc.semaphore(name)))
            return nc
        with ExitStack() as ph:
            P = ph.enter_context
            s_term = [P(nc.sbuf_tensor(f"s_term{i}", [128, NH * 128], F32)) for i in range(2)]
            s_nbig = P(nc.sbuf_tensor("s_nbig", [128, NH * 128], F32))
            s_LT = P(nc.sbuf_tensor("s_LT", [128, NH * 128], BF16))
            s_MT = P(nc.sbuf_tensor("s_MT", [128, NH * 128], BF16))
            s_ea2 = P(nc.sbuf_tensor("s_ea2", [128, D], BF16))
            s_ctil = P(nc.sbuf_tensor("s_ctil", [128, D], BF16))
            s_acsb = [P(nc.sbuf_tensor(f"s_acsb{i}", [NH, 128], BF16)) for i in range(2)]
            s_Bc = [P(nc.sbuf_tensor(f"s_Bc{i}", [128, 16 * 128], BF16)) for i in range(2)]
            s_Cc = [P(nc.sbuf_tensor(f"s_Cc{i}", [128, 16 * 128], BF16)) for i in range(2)]
            s_szc = [P(nc.sbuf_tensor(f"s_szc{i}", [128, 16 * 128], BF16)) for i in range(2)]
            s_xt2 = [P(nc.sbuf_tensor(f"s_xt2{i}", [128, D], BF16)) for i in range(2)]
            s_Pc = [P(nc.sbuf_tensor(f"s_Pc{i}", [128, 16 * HD], BF16)) for i in range(2)]
            s_Sg = P(nc.sbuf_tensor("s_Sg", [128, 16 * HD], F32))
            s_Sgt = P(nc.sbuf_tensor("s_Sgt", [128, 16 * HD], F32))
            s_Sg2 = P(nc.sbuf_tensor("s_Sg2", [128, 16 * HD], F32))
            s_sin = [P(nc.sbuf_tensor(f"s_sin{i}", [128, 16 * HD], BF16)) for i in range(2)]
            s_ecr = [P(nc.sbuf_tensor(f"s_ecr{i}", [128, 16], F32)) for i in range(2)]
            s_yg = P(nc.sbuf_tensor("s_yg", [128, D], BF16))
            s_sq2 = P(nc.sbuf_tensor("s_sq2", [128, D], BF16))
            s_yn = P(nc.sbuf_tensor("s_yn", [128, D], BF16))
            s_r1 = P(nc.sbuf_tensor("s_r1", [1, 128], F32))
            s_rb = P(nc.sbuf_tensor("s_rb", [128, D], F32))
            p_eg = P(nc.psum_tensor("p_eg", [128, 2048], F32))
            p_y = P(nc.psum_tensor("p_y", [128, 2048], F32))

            s_bo = P(nc.sbuf_tensor("s_bo", [NH, D], BF16))
            s_DI = P(nc.sbuf_tensor("s_DI", [128, NH * 128], BF16))

            bar = ffn_end
            w0 = barw(bar)
            nb_ld = dma("sync", "dNb", s_nbig[:, :], t_nbig, waits=w0)
            bo_ld = dma("sync", "dBo", s_bo[:, :], t_bo, waits=w0)
            for h in range(NH):
                op("vector", lambda eng, h=h: eng.tensor_scalar(
                    s_DI[:, h * 128:(h + 1) * 128], s_e128b[:, :], s_drep[:, h:h + 1],
                    None, ALU.mult),
                    waits=w0 + (("dCst", consts_dS), ("dCst2", consts_dA), ("cV", n["cV"])))
            di_done = n["cV"]
            sgv = 0
            for r in range(ncores):
                sld = dma("sync", "dSg", s_Sgt[:, :],
                          cc_out.rearrange("(r p) m -> r p m", p=128)[r, :, :],
                          waits=w0 + (("dC", 1), ("cV", sgv)))
                if r == 0:
                    sgv = op("vector", lambda eng: eng.tensor_scalar(
                        s_Sg[:, :], s_Sgt[:, :], s_sel[:, 0:1], None, ALU.mult),
                        waits=(("dSg", sld), ("cV", n["cV"])))
                else:
                    sgv = op("vector", lambda eng, r=r: eng.scalar_tensor_tensor(
                        s_Sg[:, :], s_Sgt[:, :], s_sel[:, r:r + 1], s_Sg[:, :],
                        ALU.mult, ALU.add), waits=(("dSg", sld), ("cV", n["cV"])))

            per_chunk = {}
            ld_user = {}

            def chunk_loads(c):
                i = c % 2
                u = ld_user.get(i, {})
                per_chunk[c] = dict(
                    b=dma("sync", f"dYb{i}", s_Bc[i][:, :].rearrange("p (fb t) -> p fb t", fb=16),
                          d_xbc.rearrange("(fb p) t -> p fb t", p=128)[:, 16:32, c * 128:(c + 1) * 128],
                          waits=w0 + (u.get("b", ("cT", 0)),)),
                    cc=dma("sync", f"dYc{i}", s_Cc[i][:, :].rearrange("p (fb t) -> p fb t", fb=16),
                           d_xbc.rearrange("(fb p) t -> p fb t", p=128)[:, 32:48, c * 128:(c + 1) * 128],
                           waits=w0 + (u.get("cc", ("cT", 0)),)),
                    sz=dma("sync", f"dYs{i}", s_szc[i][:, :].rearrange("p (fb t) -> p fb t", fb=16),
                           d_sz.rearrange("(fb p) t -> p fb t", p=128)[:, :, c * 128:(c + 1) * 128],
                           waits=w0 + (u.get("sz", ("cV", 0)),)),
                    x=dma("scalar", f"dYx{i}", s_xt2[i][:, :], d_xbc[0:D, c * 128:(c + 1) * 128],
                          waits=w0 + (u.get("x", ("cT", 0)),), transpose=True),
                    p=dma("sync", f"dYp{i}", s_Pc[i][:, :], d_P[c, :, :],
                          waits=w0 + (u.get("p", ("cV", 0)),)),
                    term=dma("scalar", f"dYt{i}", s_term[i][:, :],
                             d_acs[c, :, :].rearrange("h t -> (h t)").unsqueeze(0)
                             .broadcast_to([128, NH * 128]),
                             waits=w0 + (u.get("term", ("cA", 0)),)),
                    acs=dma("gpsimd", f"dYa{i}", s_acsb[i][:, :], d_acs[c, :, :],
                            waits=w0 + (u.get("acs", ("cT", 0)),)),
                    ec0=dma("scalar", f"dYe{i}", s_ecr[i][0:64, :],
                            d_cuma[c, :].rearrange("(pb hf) -> hf pb", hf=2)[0:1, :]
                            .broadcast_to([64, 16]),
                            waits=w0 + (u.get("ec", ("cV", 0)),), allow_slow_non_contiguous=True),
                    ec=dma("scalar", f"dYe{i}", s_ecr[i][64:128, :],
                           d_cuma[c, :].rearrange("(pb hf) -> hf pb", hf=2)[1:2, :]
                           .broadcast_to([64, 16]),
                           waits=w0 + (u.get("ec", ("cV", 0)),), allow_slow_non_contiguous=True),
                )
                return per_chunk[c]

            chunk_loads(0)
            yn_user, rd_user = 0, 0

            for c in range(nch if y_chunks is None else min(y_chunks, nch)):
                i = c % 2
                L = per_chunk[c]
                if y_sub < 1:
                    continue
                e1 = op("scalar", lambda eng, i=i: eng.activation(
                    s_ecr[i][:, :], s_ecr[i][:, :], AF.Exp),
                    waits=((f"dYe{i}", L["ec"]), ("cA", n["cA"])))
                u1 = op("vector", lambda eng, i=i: eng.tensor_tensor(
                    s_Sg2[:, :].rearrange("p (h x) -> p h x", h=16),
                    s_Sg[:, :].rearrange("p (h x) -> p h x", h=16),
                    s_ecr[i][:, :].unsqueeze(2).broadcast_to([128, 16, HD]), ALU.mult),
                    waits=(("cA", e1), ("cV", max(sgv, n["cV"]))))
                sv = op("vector", lambda eng, i=i: eng.tensor_add(
                    s_sin[i][:, :], s_Sg2[:, :], s_Pc[i][:, :]),
                    waits=(("cV", u1), (f"dYp{i}", L["p"])))
                if y_sub < 2:
                    continue
                t1 = op("vector", lambda eng, i=i, c=c: eng.tensor_tensor(
                    s_term[i][:, :].rearrange("p (h t) -> p h t", h=NH),
                    s_term[i][:, :].rearrange("p (h t) -> p h t", h=NH),
                    s_acstm[:, c * 32:(c + 1) * 32].unsqueeze(2).broadcast_to([128, NH, 128]),
                    ALU.subtract), waits=((f"dYt{i}", L["term"]), ("cV", n["cV"])))
                t2 = op("vector", lambda eng, i=i: eng.scalar_tensor_tensor(
                    s_term[i][:, :], s_term[i][:, :], 87.0, s_nbig[:, :], ALU.min, ALU.add),
                    waits=(("cV", t1), ("dNb", nb_ld)))
                lt = op("scalar", lambda eng, i=i: eng.activation(
                    s_LT[:, :], s_term[i][:, :], AF.Exp), waits=(("cV", t2), ("cA", n["cA"])))
                if y_sub < 3:
                    continue
                for pb in range(16):
                    wts = ((f"dYa{i}", L["acs"]), ("cV", n["cV"]), ("cA", n["cA"]),
                           ("dBo", bo_ld), ("cT", n["cT"])) if pb == 0 else ()
                    op("tensor", lambda eng, pb=pb, i=i: eng.matmul(
                        p_eg[:, pb * 128:(pb + 1) * 128],
                        s_bo[:, pb * 128:(pb + 1) * 128], s_acsb[i][:, :],
                        start=True, stop=True), waits=wts)
                ea_mm = n["cT"]
                eav = op("scalar", lambda eng: eng.activation(
                    s_ea2[:, :], p_eg[:, :], AF.Exp), waits=(("cT", ea_mm), ("cA", n["cA"])))
                ctv = op("vector", lambda eng, i=i: eng.tensor_mul(
                    s_ctil[:, :], s_Cc[i][:, :], s_ea2[:, :]),
                    waits=(("cA", eav), ((f"dYc{i}"), L["cc"]), ("cV", n["cV"])))
                if y_sub < 4:
                    continue
                mvals = []
                for grp in range(2):
                    first = True
                    for half in range(2):
                        for pbi in range(8):
                            pb = grp * 8 + pbi
                            sl = half * 8 + pbi
                            wts = ((f"dYb{i}", L["b"]), ("cA", eav), ("cV", n["cV"]),
                                   ("cT", n["cT"])) if first else ()
                            first = False
                            op("tensor", lambda eng, i=i, pb=pb, half=half, sl=sl:
                               eng.matmul(
                                   p_eg[:, sl * 128:(sl + 1) * 128],
                                   s_Bc[i][half * 64:half * 64 + 64, pb * 128:(pb + 1) * 128],
                                   s_Cc[i][half * 64:half * 64 + 64, pb * 128:(pb + 1) * 128],
                                   start=True, stop=True), waits=wts)
                    gmm = n["cT"]
                    if y_sub >= 45:
                        for half in range(2):
                            mv = op("vector", lambda eng, grp=grp, half=half: eng.tensor_mul(
                                s_MT[:, grp * 2048:(grp + 1) * 2048].rearrange(
                                    "p (pb half t) -> p half pb t", pb=8, half=2)[:, half],
                                p_eg[:, half * 1024:(half + 1) * 1024].rearrange(
                                    "p (pb t) -> p pb t", pb=8),
                                s_LT[:, grp * 2048:(grp + 1) * 2048].rearrange(
                                    "p (pb half t) -> p half pb t", pb=8, half=2)[:, half]),
                                waits=(("cT", gmm), ("cA", lt), ("cV", n["cV"])))
                            mvals.append(mv)
                    else:
                        mvals.append(0)
                if y_sub < 5:
                    continue
                for h in range(NH):
                    pb, half = h // 2, h % 2
                    wts = ()
                    if h == 0:
                        wts = (("cV", max(mvals)), ("cV", ctv), ("cV", sv),
                               (f"dYx{i}", L["x"]), ("cV", di_done), ("cT", n["cT"]))
                    op("tensor", lambda eng, i=i, h=h, pb=pb, half=half: eng.matmul(
                        p_y[half * 64:half * 64 + 64, pb * 128:(pb + 1) * 128],
                        s_xt2[i][:, h * 64:(h + 1) * 64],
                        s_MT[:, h * 128:(h + 1) * 128], start=True, stop=False),
                        waits=wts)
                    op("tensor", lambda eng, i=i, h=h, pb=pb, half=half: eng.matmul(
                        p_y[half * 64:half * 64 + 64, pb * 128:(pb + 1) * 128],
                        s_xt2[i][:, h * 64:(h + 1) * 64],
                        s_DI[:, h * 128:(h + 1) * 128], start=False, stop=False))
                    op("tensor", lambda eng, i=i, h=h, pb=pb, half=half: eng.matmul(
                        p_y[half * 64:half * 64 + 64, pb * 128:(pb + 1) * 128],
                        s_sin[i][half * 64:half * 64 + 64, pb * HD:(pb + 1) * HD],
                        s_ctil[:, :].rearrange("p (fb t) -> p fb t", fb=16)[half * 64:half * 64 + 64, pb, :],
                        start=False, stop=True))
                y_mm = n["cT"]
                if y_sub < 6:
                    continue
                ygv = op("vector", lambda eng, i=i: eng.tensor_mul(
                    s_yg[:, :], p_y[:, :], s_szc[i][:, :]),
                    waits=(("cT", y_mm), ((f"dYs{i}"), L["sz"]), ("cV", n["cV"])))
                sqv = op("scalar", lambda eng: eng.activation(
                    s_sq2[:, :], s_yg[:, :], AF.Square), waits=(("cV", ygv), ("cA", n["cA"])))
                for pb in range(16):
                    wts = (("cA", sqv), ("cV", max(mvals)), ("cT", n["cT"])) if pb == 0 else ()
                    op("tensor", lambda eng, pb=pb: eng.matmul(
                        p_eg[0:1, 0:128], s_ones[:, :], s_sq2[:, pb * 128:(pb + 1) * 128],
                        start=(pb == 0), stop=(pb == 15)), waits=wts)
                stat_mm = n["cT"]
                rv1 = op("vector", lambda eng: eng.tensor_scalar(
                    s_r1[:, :], p_eg[0:1, 0:128], 1.0 / D, 1e-5, ALU.mult, ALU.add),
                    waits=(("cT", stat_mm), ("cV", n["cV"]), ("dR", rd_user)))
                rs = op("scalar", lambda eng: eng.activation(
                    s_r1[:, :], s_r1[:, :], AF.Sqrt), waits=(("cV", rv1), ("cA", n["cA"])))
                rv2 = op("vector", lambda eng: eng.reciprocal(
                    s_r1[:, :], s_r1[:, :]), waits=(("cA", rs), ("cV", n["cV"])))
                rd = dma("sync", "dR", d_rstd[c * 128:(c + 1) * 128].unsqueeze(0),
                         s_r1[:, :], waits=(("cV", rv2),))
                rd_user = rd
                rb = dma("scalar", "dR", s_rb[:, :].rearrange("p (fb t) -> p fb t", fb=16),
                         d_rstd[c * 128:(c + 1) * 128].unsqueeze(0).unsqueeze(0)
                         .broadcast_to([128, 16, 128]), waits=(("dR", rd),))
                ynv = op("vector", lambda eng: eng.tensor_mul(
                    s_yn[:, :], s_yg[:, :], s_rb[:, :]),
                    waits=(("dR", rb), ("cV", n["cV"]), ("dYn", yn_user)))
                yn_user = nn("dYn") + 16
                dma("sync", "dYn",
                    d_yn.rearrange("(fb p) t -> p fb t", p=128)[:, :, c * 128:(c + 1) * 128],
                    s_yn[:, :].rearrange("p (fb t) -> p fb t", fb=16), waits=(("cV", ynv),))
                ld_user[i] = dict(b=("cT", y_mm), cc=("cV", ctv), sz=("cV", ygv),
                                  x=("cT", y_mm), p=("cV", sv), term=("cA", lt),
                                  acs=("cT", ea_mm), ec=("cV", u1))
                if c + 1 < nch:
                    chunk_loads(c + 1)
            ypass_end = S.all_state()

        if stop_after == "ypass":
            S.barrier("sync"); S.barrier("gpsimd"); S.barrier("scalar")
            S.flush(blk, lambda name: E(nc.semaphore(name)))
            return nc
        with ExitStack() as ph:
            P = ph.enter_context
            s_wo = P(nc.sbuf_tensor("s_wo", [128, 16 * D], BF16))
            s_ynl = P(nc.sbuf_tensor("s_ynl", [128, 16 * T], BF16))
            s_gsl = [P(nc.sbuf_tensor(f"s_gsl{i}", [128, T], BF16)) for i in range(2)]
            s_tst = [P(nc.sbuf_tensor(f"s_tst{i}", [128, T], BF16)) for i in range(2)]
            s_xf = [P(nc.sbuf_tensor(f"s_xf{i}", [128, D], F32)) for i in range(2)]
            s_tt = [P(nc.sbuf_tensor(f"s_tt{i}", [128, D], BF16)) for i in range(2)]
            s_of = [P(nc.sbuf_tensor(f"s_of{i}", [128, D], F32)) for i in range(2)]
            p_o = [P(nc.psum_tensor(f"p_o{i}", [128, 512], F32)) for i in range(2)]

            bar = ypass_end
            w0 = barw(bar)
            wo_ld = dma("gpsimd", "dWo", s_wo[:, :].rearrange("p (ks m) -> p ks m", ks=16),
                        t_woutT.rearrange("(ks kp) m -> kp ks m", kp=128), waits=w0)
            rmv = []
            for ks in range(16):
                rmv.append(op("vector", lambda eng, ks=ks: eng.tensor_scalar(
                    s_wo[:, ks * D:(ks + 1) * D], s_wo[:, ks * D:(ks + 1) * D],
                    s_rw[:, ks:ks + 1], None, ALU.mult),
                    waits=(("dWo", wo_ld), ("cV", n["cV"])) + w0))
            wo_done = n["cV"]
            yn_ld = []
            for ks in range(16):
                yn_ld.append(dma("sync", "dYnRd", s_ynl[:, ks * T:(ks + 1) * T],
                                 d_yn[ks * 128:(ks + 1) * 128, :], waits=w0))
            yn_dS = nn("dYnRd")
            g_ld = {}
            g_user = {}
            for db in range(min(2, 16)):
                g_ld[db] = dma("sync", f"dGld{db % 2}", s_gsl[db % 2][:, :],
                               d_g[db * 128:(db + 1) * 128, :], waits=w0)
            tspill = []
            tst_user = {}
            for db in range(16):
                i = db % 2
                for ti, (q0, q1) in enumerate(spans(0, T, 512)):
                    nw = q1 - q0
                    po = p_o[ti % 2]
                    for ks in range(16):
                        wts = (("cV", wo_done), ("dYnRd", yn_dS),
                               ("cV", n["cV"])) if ks == 0 else ()
                        op("tensor", lambda eng, po=po, ks=ks, db=db, q0=q0, q1=q1, nw=nw:
                           eng.matmul(po[:, 0:nw],
                                      s_wo[:, :].rearrange("p (ks m) -> p ks m", ks=16)[:, ks, db * 128:(db + 1) * 128],
                                      s_ynl[:, :].rearrange("p (ks t) -> p ks t", ks=16)[:, ks, q0:q1],
                                      start=(ks == 0), stop=(ks == 15)), waits=wts)
                    omm = n["cT"]
                    op("vector", lambda eng, po=po, i=i, q0=q0, nw=nw, db=db:
                       eng.tensor_mul(s_tst[i][:, q0:q0 + nw], po[:, 0:nw],
                                      s_gsl[i][:, q0:q0 + nw]),
                       waits=(("cT", omm), ((f"dGld{db % 2}"), g_ld[db]),
                              tst_user.get(i, ("cV", 0)), ("cV", n["cV"])))
                tv = n["cV"]
                td = dma("sync", f"dTs{i}", d_t[db * 128:(db + 1) * 128, :], s_tst[i][:, :],
                         waits=(("cV", tv),))
                tspill.append(td)
                tst_user[i] = (f"dTs{i}", td)
                if db + 2 < 16:
                    g_ld[db + 2] = dma("sync", f"dGld{i}", s_gsl[i][:, :],
                                       d_g[(db + 2) * 128:(db + 3) * 128, :],
                                       waits=(("cV", tv),))
            t_done = (("dTs0", nn("dTs0")), ("dTs1", nn("dTs1")))
            of_user = {}
            out_dmas = []
            for tb in range(T // 128):
                i = tb % 2
                xl = dma("sync", f"dXf{i}", s_xf[i][:, :], t_x[tb * 128:(tb + 1) * 128, :],
                         waits=(of_user.get(i, ("cV", 0)),))
                tl = dma("scalar", f"dTt{i}", s_tt[i][:, :], d_t[:, tb * 128:(tb + 1) * 128],
                         waits=t_done + (of_user.get(i, ("cV", 0)),), transpose=True)
                ov = op("vector", lambda eng, i=i: eng.tensor_add(
                    s_of[i][:, :], s_xf[i][:, :], s_tt[i][:, :]),
                    waits=((f"dXf{i}", xl), (f"dTt{i}", tl), ("cV", n["cV"])))
                od = dma("sync", f"dOut{i}", t_out[tb * 128:(tb + 1) * 128, :], s_of[i][:, :],
                         waits=(("cV", ov),))
                out_dmas.append(od)
                of_user[i] = (f"dOut{i}", od)
            S.barrier("sync")
            S.barrier("gpsimd")

        S.flush(blk, lambda name: E(nc.semaphore(name)))
    return nc



def _consts():
    negbig = np.zeros((128, NH * 128), np.float32)
    sidx = np.arange(128)[:, None]
    tidx = np.arange(128)[None, :]
    m = np.where(sidx > tidx, np.float32(NEGBIG), np.float32(0.0))
    negbig = np.tile(m, (1, NH)).astype(np.float32)
    bo = np.zeros((NH, D), ml_dtypes.bfloat16)
    for pb in range(16):
        for q in range(128):
            bo[2 * pb + q // 64, pb * 128 + q] = 1
    return dict(
        negbig=np.ascontiguousarray(negbig),
        blockones=np.ascontiguousarray(bo),
        ones128b=np.ones((128, 1), ml_dtypes.bfloat16),
        eye128b=np.eye(128, dtype=ml_dtypes.bfloat16),
        eye128f=np.eye(128, dtype=np.float32),
        eye32f=np.eye(32, dtype=np.float32),
        eye3b=np.eye(3, dtype=ml_dtypes.bfloat16),
    )


def make_inmaps(ins, T=2048, ncores=NCORE):
    U = np.asarray(ins["L0_attn_u"], np.float32)
    consts = _consts()
    shared = dict(
        winT=np.ascontiguousarray(np.asarray(ins["L0_attn_W_in"], np.float32).T),
        woutT=np.ascontiguousarray(np.asarray(ins["L0_attn_W_out"], np.float32).T),
        wgT=np.ascontiguousarray(np.asarray(ins["L0_ffn_W_g"], np.float32).T),
        wuT=np.ascontiguousarray(np.asarray(ins["L0_ffn_W_u"], np.float32).T),
        wdT=np.ascontiguousarray(np.asarray(ins["L0_ffn_W_d"], np.float32).T),
        zbias_pm=np.ascontiguousarray(
            np.asarray(ins["L0_attn_z_bias"], np.float32).reshape(16, 128).T),
        convw_pm=np.ascontiguousarray(
            np.asarray(ins["L0_attn_conv_w"], np.float32).reshape(48, 128, 4)
            .transpose(1, 0, 2).reshape(128, 192)),
        convb_pm=np.ascontiguousarray(
            np.asarray(ins["L0_attn_conv_b"], np.float32).reshape(48, 128).T),
        rmsw_pm=np.ascontiguousarray(
            np.asarray(ins["L0_attn_rms_w"], np.float32).reshape(16, 128).T),
        dvec=np.ascontiguousarray(np.asarray(ins["L0_attn_D"], np.float32)),
        **consts,
    )
    in_maps = []
    for core in range(ncores):
        b, h = core // 2, core % 2
        x = np.ascontiguousarray(U[b, h * T:(h + 1) * T, :])
        halo = (np.zeros((3, D), np.float32) if h == 0
                else np.ascontiguousarray(U[b, h * T - 3:h * T, :]))
        sel = np.zeros((128, ncores), np.float32)
        if h == 1:
            sel[:, (core - 1) % ncores] = 1.0
        in_maps.append(dict(x=x, x_halo=halo, sel=sel, **shared))
    return in_maps


def kernel(**inputs):
    T = 2048
    nc = build_program(T=T)
    in_maps = make_inmaps(inputs, T=T)
    res = run_bass_kernel_spmd(nc, in_maps, core_ids=list(range(NCORE)))
    B, L = 4, 4096
    out = np.empty((B, L, D), np.float32)
    for core in range(NCORE):
        b, h = core // 2, core % 2
        out[b, h * T:(h + 1) * T, :] = res.results[core]["out"]
    return out


# version 03 (final)
# speedup vs baseline: 1.0688x; 1.0688x over previous
import numpy as np
import ml_dtypes
from contextlib import ExitStack

import concourse.bass as bass
import concourse.mybir as mybir
from concourse.bass_utils import run_bass_kernel_spmd

F32 = mybir.dt.float32
BF16 = mybir.dt.bfloat16
AF = mybir.ActivationFunctionType
ALU = mybir.AluOpType
AX = mybir.AxisListType

D = 2048
NH = 32
DS = 64
HD = 64
CONVD = 6144
PROJ = 8224
HID = 8192
NCORE = 8
C23 = float(1.5 * 2.0**23)
C52 = float(1.5 * 2.0**52)
NEGBIG = -1e30
XOFF = 16

ENGS = ("sync", "scalar", "vector", "tensor", "gpsimd")
QSEM = {"sync": "dS", "scalar": "dA", "gpsimd": "dG"}


def spans(lo, hi, step):
    return [(s, min(s + step, hi)) for s in range(lo, hi, step)]


class Sched:

    def __init__(self):
        self.q = {e: [] for e in ENGS}
        self.n = {}
        self.hw = {e: {} for e in ENGS}

    def emit(self, eng_name, build, waits=(), inc=None, k=1):
        need = []
        hw = self.hw[eng_name]
        for s, v in waits:
            if v > hw.get(s, 0):
                need.append((s, v))
                hw[s] = v
            self.n.setdefault(s, 0)
        if inc is not None:
            self.n[inc] = self.n.get(inc, 0) + k
        self.q[eng_name].append((tuple(need), build, inc, k))
        return self.n[inc] if inc is not None else None

    def barrier(self, eng_name, sems=None):
        sems = sems if sems is not None else list(self.n)
        waits = tuple((s, self.n[s]) for s in sems if self.n.get(s, 0) > 0)
        self.emit(eng_name, None, waits)

    def all_state(self):
        return dict(self.n)

    def flush(self, blk, mksem):
        semobjs = {k: mksem(k) for k in self.n}
        for e in ENGS:
            items = self.q[e]
            if not items:
                continue

            def run(eng, items=items):
                for need, build, inc, k in items:
                    for s, v in need:
                        eng.wait_ge(semobjs[s], v)
                    if build is None:
                        continue
                    ins = build(eng)
                    if inc is not None:
                        ins.then_inc(semobjs[inc], k)

            getattr(blk, e)(run)


def build_program(T=2048, debug=(), ncores=NCORE, stop_after=None,
                  y_sub=99, y_chunks=None):
    nch = T // 128
    TH = T // 2
    nc = bass.Bass("TRN2", target_bir_lowering=False)

    def din(name, shape, dt=F32):
        return nc.dram_tensor(name, shape, dt, kind="ExternalInput").ap()

    t_x = din("x", [T, D])
    t_xh = din("x_halo", [3, D])
    t_sel = din("sel", [128, ncores])
    t_winT = din("winT", [D, PROJ])
    t_woutT = din("woutT", [D, D])
    t_wgT = din("wgT", [D, HID])
    t_wuT = din("wuT", [D, HID])
    t_wdT = din("wdT", [HID, D])
    t_zb = din("zbias_pm", [128, 16])
    t_cw = din("convw_pm", [128, 48 * 4])
    t_cb = din("convb_pm", [128, 48])
    t_rw = din("rmsw_pm", [128, 16])
    t_dv = din("dvec", [NH])
    t_bo = din("blockones", [NH, D], BF16)
    t_ones = din("ones128b", [128, 1], BF16)
    t_e128b = din("eye128b", [128, 128], BF16)
    t_e128f = din("eye128f", [128, 128])
    t_e32f = din("eye32f", [32, 32])
    t_e3b = din("eye3b", [3, 3], BF16)
    t_nbig = din("negbig", [128, 4096])

    t_out = nc.dram_tensor("out", [T, D], F32, kind="ExternalOutput").ap()

    def scr(name, shape, dt):
        kind = "ExternalOutput" if name in debug else "Internal"
        return nc.dram_tensor(name, shape, dt, kind=kind).ap()

    d_xn = scr("d_xn", [T, D], BF16)
    d_xbc = scr("d_xbc", [CONVD, T], BF16)
    d_sz = scr("d_sz", [D, T], BF16)
    d_P = scr("d_P", [nch, 128, 16 * HD], BF16)
    d_acs = scr("d_acs", [nch, NH, 128], F32)
    d_cuma = scr("d_cuma", [nch, NH], F32)
    d_atot = scr("d_atot", [nch, NH], F32)
    d_srow = scr("d_srow", [2, T], F32)
    d_rstd = scr("d_rstd", [T], F32)
    d_g = scr("d_g", [D, T], BF16)
    d_yn = scr("d_yn", [D, T], BF16)
    d_t = scr("d_t", [D, T], BF16)
    d_xq = scr("d_xq", [16 * 128, T], BF16)
    d_wg = scr("d_wg", [128, 16 * 256], BF16)
    d_p = scr("d_p", [128, 2 * (T // 2)], BF16)
    d_f = scr("d_f", [16 * 128, T], BF16)
    cc_in = scr("cc_in", [128, 16 * HD], F32)
    cc_out = scr("cc_out", [128 * ncores, 16 * HD], F32)

    S = Sched()

    def dma(q, sem, out, in_, waits=(), **kw):
        return S.emit(q, (lambda eng, o=out, i=in_, kw=dict(kw): eng.dma_start(out=o, in_=i, **kw)),
                      waits, sem, 16)

    def op(e, build, waits=()):
        sem = {"vector": "cV", "scalar": "cA", "gpsimd": "cP", "tensor": "cT"}[e]
        return S.emit(e, build, waits, sem, 1)

    def nn(name):
        return S.n.get(name, 0)

    def barw(bar):
        return tuple((k, v) for k, v in bar.items() if v > 0)
    n = S.n
    for _k in ("cV", "cA", "cP", "cT"):
        n[_k] = 0

    with ExitStack() as top:
        E = top.enter_context
        blk = E(nc.Block())

        s_cw = E(nc.sbuf_tensor("s_cw", [128, 192], F32))
        s_cb = E(nc.sbuf_tensor("s_cb", [128, 48], F32))
        s_zb = E(nc.sbuf_tensor("s_zb", [128, 16], F32))
        s_rw = E(nc.sbuf_tensor("s_rw", [128, 16], F32))
        s_sel = E(nc.sbuf_tensor("s_sel", [128, ncores], F32))
        s_ones = E(nc.sbuf_tensor("s_ones", [128, 1], BF16))
        s_e128b = E(nc.sbuf_tensor("s_e128b", [128, 128], BF16))
        s_e128f = E(nc.sbuf_tensor("s_e128f", [128, 128], F32))
        s_e32f = E(nc.sbuf_tensor("s_e32f", [32, 32], F32))
        s_e3b = E(nc.sbuf_tensor("s_e3b", [3, 3], BF16))
        s_drep = E(nc.sbuf_tensor("s_drep", [128, NH], F32))
        s_flag = E(nc.sbuf_tensor("s_flag", [128, 2], F32))
        s_eps = E(nc.sbuf_tensor("s_eps", [128, 1], F32))
        s_var = E(nc.sbuf_tensor("s_var", [128, 16], F32))
        s_mxn = E(nc.sbuf_tensor("s_mxn", [128, 16], F32))
        s_acstm = E(nc.sbuf_tensor("s_acstm", [128, nch * 32], F32))
        ph123 = ExitStack()
        s_Ahm = ph123.enter_context(nc.sbuf_tensor("s_Ahm", [NH, T], F32))
        s_xhn = ph123.enter_context(nc.sbuf_tensor("s_xhn", [3, D], BF16))

        for dst, src in ((s_cw, t_cw), (s_cb, t_cb), (s_zb, t_zb), (s_rw, t_rw),
                         (s_sel, t_sel), (s_ones, t_ones), (s_e128b, t_e128b),
                         (s_e128f, t_e128f), (s_e32f, t_e32f), (s_e3b, t_e3b)):
            dma("sync", "dCst", dst[:, :], src)
        dma("scalar", "dCst2", s_drep[:, :], t_dv.unsqueeze(0).broadcast_to([128, NH]))
        consts_dS, consts_dA = nn("dCst"), nn("dCst2")
        for _e in ENGS:
            S.emit(_e, None, waits=(("dCst", consts_dS), ("dCst2", consts_dA)))

        op("vector", lambda eng: eng.memset(s_eps[:, :], 1e-5))
        eps_done = n["cV"]
        op("vector", lambda eng: eng.memset(s_flag[:, 0:1], 1.0))
        op("vector", lambda eng: eng.tensor_scalar(
            s_flag[:, 1:2], s_flag[:, 0:1], float(2.0**-30), -1.0, ALU.add, ALU.add),
            waits=(("cV", n["cV"]),))
        op("vector", lambda eng: eng.tensor_scalar(
            s_flag[:, 0:1], s_flag[:, 1:2], 0.0, None, ALU.is_gt),
            waits=(("cV", n["cV"]),))
        flag_done = n["cV"]

        with ExitStack() as ph:
            P = ph.enter_context
            s_x = [P(nc.sbuf_tensor(f"s_x{i}", [128, D], F32)) for i in range(2)]
            s_sq = P(nc.sbuf_tensor("s_sq", [128, D], F32))
            s_xnst = [P(nc.sbuf_tensor(f"s_xnst{i}", [128, D], BF16)) for i in range(2)]
            s_st = P(nc.sbuf_tensor("s_st", [128, 8], F32))
            s_xh = P(nc.sbuf_tensor("s_xh", [3, D], F32))
            s_hst = P(nc.sbuf_tensor("s_hst", [3, 8], F32))
            s_hsq = P(nc.sbuf_tensor("s_hsq", [3, D], F32))

            ld = {}
            done_v = {}
            for tb in range(min(2, T // 128)):
                ld[tb] = dma("sync", f"dLx{tb % 2}", s_x[tb % 2][:, :],
                             t_x[tb * 128:(tb + 1) * 128, :])
            st_dma = []
            for tb in range(T // 128):
                i = tb % 2
                xb = s_x[i]
                w = ((f"dLx{tb % 2}", ld[tb]),)
                op("vector", lambda eng, xb=xb: eng.reduce_sum(
                    s_st[:, 0:1], xb[:, :], AX.X), waits=w + (("cV", n["cV"]),))
                op("vector", lambda eng, xb=xb: eng.scalar_tensor_tensor(
                    s_sq[:, :], xb[:, :], 1.0, xb[:, :], ALU.mult, ALU.mult,
                    accum_out=s_st[:, 1:2]), waits=(("cV", n["cV"]),))
                op("vector", lambda eng: eng.tensor_scalar(
                    s_st[:, 2:3], s_st[:, 0:1], 1.0 / D, None, ALU.mult),
                    waits=(("cV", n["cV"]),))
                op("vector", lambda eng: eng.tensor_scalar(
                    s_st[:, 3:4], s_st[:, 1:2], 1.0 / D, None, ALU.mult),
                    waits=(("cV", n["cV"]),))
                op("vector", lambda eng: eng.scalar_tensor_tensor(
                    s_st[:, 4:5], s_st[:, 2:3], -1.0, s_st[:, 2:3], ALU.mult, ALU.mult),
                    waits=(("cV", n["cV"]),))
                op("vector", lambda eng, tb=tb: eng.scalar_tensor_tensor(
                    s_var[:, tb:tb + 1], s_st[:, 3:4], 1.0, s_st[:, 4:5], ALU.mult, ALU.add),
                    waits=(("cV", n["cV"]),))
                op("scalar", lambda eng, tb=tb: eng.activation(
                    s_st[:, 5:6], s_var[:, tb:tb + 1], AF.Sqrt, bias=s_eps[:, 0:1]),
                    waits=(("cV", n["cV"]), ("cV", eps_done)))
                op("vector", lambda eng: eng.reciprocal(
                    s_st[:, 6:7], s_st[:, 5:6]), waits=(("cA", n["cA"]), ("cV", n["cV"])))
                op("vector", lambda eng, xb=xb, i=i: eng.tensor_scalar(
                    s_xnst[i][:, :], xb[:, :], s_st[:, 2:3], s_st[:, 6:7],
                    ALU.subtract, ALU.mult),
                    waits=(("cV", n["cV"]), ("dXnSp", st_dma[-1] if st_dma else 0)))
                xn_ready = n["cV"]
                op("vector", lambda eng, tb=tb, i=i: eng.reduce_max(
                    s_mxn[:, tb:tb + 1], s_xnst[i][:, :], AX.X, apply_absolute_value=True),
                    waits=(("cV", xn_ready),))
                done_v[i] = n["cV"]
                st_dma.append(dma("sync", "dXnSp", d_xn[tb * 128:(tb + 1) * 128, :],
                                  s_xnst[i][:, :], waits=(("cV", xn_ready),)))
                if tb + 2 < T // 128:
                    ld[tb + 2] = dma("sync", f"dLx{i}", s_x[i][:, :],
                                     t_x[(tb + 2) * 128:(tb + 3) * 128, :],
                                     waits=(("cV", done_v[i]),))
            hld = dma("sync", "dHalo", s_xh[:, :], t_xh)
            op("vector", lambda eng: eng.reduce_sum(s_hst[:, 0:1], s_xh[:, :], AX.X),
               waits=(("dHalo", hld), ("cV", n["cV"])))
            op("vector", lambda eng: eng.scalar_tensor_tensor(
                s_hsq[:, :], s_xh[:, :], 1.0, s_xh[:, :], ALU.mult, ALU.mult,
                accum_out=s_hst[:, 1:2]), waits=(("cV", n["cV"]),))
            op("vector", lambda eng: eng.tensor_scalar(
                s_hst[:, 2:3], s_hst[:, 0:1], 1.0 / D, None, ALU.mult), waits=(("cV", n["cV"]),))
            op("vector", lambda eng: eng.tensor_scalar(
                s_hst[:, 3:4], s_hst[:, 1:2], 1.0 / D, None, ALU.mult), waits=(("cV", n["cV"]),))
            op("vector", lambda eng: eng.scalar_tensor_tensor(
                s_hst[:, 4:5], s_hst[:, 2:3], -1.0, s_hst[:, 2:3], ALU.mult, ALU.mult),
                waits=(("cV", n["cV"]),))
            op("vector", lambda eng: eng.scalar_tensor_tensor(
                s_hst[:, 5:6], s_hst[:, 3:4], 1.0, s_hst[:, 4:5], ALU.mult, ALU.add),
                waits=(("cV", n["cV"]),))
            op("scalar", lambda eng: eng.activation(
                s_hst[:, 6:7], s_hst[:, 5:6], AF.Sqrt, bias=s_eps[0:3, 0:1]),
                waits=(("cV", n["cV"]),))
            op("vector", lambda eng: eng.reciprocal(s_hst[:, 7:8], s_hst[:, 6:7]),
               waits=(("cA", n["cA"]), ("cV", n["cV"])))
            op("vector", lambda eng: eng.tensor_scalar(
                s_xhn[:, :], s_xh[:, :], s_hst[:, 2:3], s_hst[:, 7:8],
                ALU.subtract, ALU.mult), waits=(("cV", n["cV"]),))
            halo_v = n["cV"]
            ln_sp, ln_cV = nn("dXnSp"), n["cV"]

        with ExitStack() as ph:
            P = ph.enter_context
            EXT = T + XOFF
            s_xnfm = P(nc.sbuf_tensor("s_xnfm", [128, 16 * EXT], BF16))
            s_wb = [P(nc.sbuf_tensor(f"s_wb{i}", [128, 16 * 512], BF16)) for i in range(2)]
            s_raw = P(nc.sbuf_tensor("s_raw", [128, 4 * (T + 3)], BF16))
            s_acc = [P(nc.sbuf_tensor(f"s_acc{i}", [128, T], F32)) for i in range(2)]
            s_stg = [P(nc.sbuf_tensor(f"s_stg{i}", [128, T], BF16)) for i in range(4)]
            s_Atmp = P(nc.sbuf_tensor("s_At", [NH, T], F32))
            pacc = [P(nc.psum_tensor(f"pacc{i}", [128, 512], F32)) for i in range(4)]
            ph3 = P(nc.psum_tensor("ph3", [128, 16], BF16))

            ln_bar = barw(S.all_state())
            xnfm_ld = []
            for ks in range(16):
                xnfm_ld.append(dma(
                    "scalar", "dXnRd", s_xnfm[:, ks * EXT + XOFF: ks * EXT + XOFF + T],
                    d_xn[:, ks * 128:(ks + 1) * 128], waits=ln_bar,
                    transpose=True))
            ph3_ev = 0
            for ks in range(16):
                tt = op("tensor", lambda eng, ks=ks: eng.transpose(
                    ph3[:, 0:3], s_xhn[0:3, ks * 128:(ks + 1) * 128], s_e3b[:, :]),
                    waits=ln_bar + (("cV", max(halo_v, ph3_ev)), ("cT", n["cT"])))
                ph3_ev = op("vector", lambda eng, ks=ks: eng.tensor_copy(
                    s_xnfm[:, ks * EXT + XOFF - 3: ks * EXT + XOFF], ph3[:, 0:3]),
                    waits=(("cT", tt), ("cV", n["cV"])))
            xnext_dA = nn("dXnRd")
            xnext_cV = n["cV"]

            groups = [(g * 512, 512) for g in range(16)] + [(8192, 32)]
            wload = {}

            def load_group(gi, waits=()):
                mlo, msz = groups[gi]
                sl = gi % 2
                srcw = t_winT.rearrange("(ks kp) m -> kp ks m", kp=128)[:, :, mlo:mlo + msz]
                wload[gi] = dma("gpsimd", f"dWin{sl}", s_wb[sl][:, 0:16 * msz].rearrange(
                    "p (ks m) -> p ks m", ks=16), srcw, waits=waits)

            load_group(0, waits=ln_bar)
            load_group(1, waits=ln_bar)

            raw_free = [0] * 4
            raw_free_sem = ["cV"] * 8
            prev_acc_user = {}
            stg_user = {}
            A_evac = []

            for gi, (mlo, msz) in enumerate(groups):
                sl = gi % 2
                wv = ((f"dWin{gi % 2}", wload[gi]),)
                is_xbc = gi < 12
                is_z = gi in (12, 13, 14, 15)
                is_A = not (is_xbc or is_z)
                nmb = msz // 128 if msz >= 128 else 1
                for mb in range(nmb):
                    sp = spans(0, T + 3, 512) if is_xbc else spans(0, T, 512)
                    for si, (j0, j1) in enumerate(sp):
                        pslot = (mb * len(sp) + si) % 4
                        pt = pacc[pslot]
                        nw = j1 - j0
                        np_ = 32 if is_A else 128
                        pw = prev_acc_user.get(pslot, ("cV", 0))
                        for ks in range(16):
                            if is_xbc:
                                rhs = s_xnfm[:, ks * EXT + XOFF - 3 + j0: ks * EXT + XOFF - 3 + j1]
                            else:
                                rhs = s_xnfm[:, ks * EXT + XOFF + j0: ks * EXT + XOFF + j1]
                            lhsT = s_wb[sl][:, 0:16 * msz].rearrange(
                                "p (ks m) -> p ks m", ks=16)[:, ks,
                                mb * 128:mb * 128 + (128 if msz >= 128 else msz)]
                            wts = (wv + (pw,)) if ks == 0 else ()
                            op("tensor", lambda eng, pt=pt, lhsT=lhsT, rhs=rhs, ks=ks,
                               nw=nw, np_=np_:
                               eng.matmul(pt[0:np_, 0:nw], lhsT, rhs,
                                          start=(ks == 0), stop=(ks == 15)),
                               waits=wts + (("dXnRd", xnext_dA), ("cV", xnext_cV)))
                        mm_done = n["cT"]
                        if is_xbc:
                            dst = s_raw[:, mb * (T + 3) + j0: mb * (T + 3) + j1]
                            ev = op("vector", lambda eng, dst=dst, pt=pt, nw=nw:
                                    eng.tensor_copy(dst, pt[:, 0:nw]),
                                    waits=(("cT", mm_done),
                                           (raw_free_sem[mb], raw_free[mb]),
                                           ("cV", n["cV"])))
                            prev_acc_user[pslot] = ("cV", ev)
                        elif is_z:
                            zb = (gi - 12) * 4 + mb
                            stg = s_stg[mb % 2]
                            su = stg_user.get(mb % 2, ("cV", 0))
                            ev = op("scalar", lambda eng, stg=stg, pt=pt, j0=j0, nw=nw, zb=zb:
                                    eng.activation(stg[:, j0:j0 + nw], pt[:, 0:nw], AF.Silu,
                                                   bias=s_zb[:, zb:zb + 1]),
                                    waits=(("cT", mm_done), su, ("cA", n["cA"])))
                            prev_acc_user[pslot] = ("cA", ev)
                            if si == len(sp) - 1:
                                sd = dma("sync", f"dSz{mb % 2}",
                                         d_sz[zb * 128:(zb + 1) * 128, :], stg[:, :],
                                         waits=(("cA", ev),))
                                stg_user[mb % 2] = (f"dSz{mb % 2}", sd)
                        else:
                            ev = op("vector", lambda eng, pt=pt, j0=j0, nw=nw:
                                    eng.tensor_copy(s_Ahm[0:NH, j0:j0 + nw], pt[0:NH, 0:nw]),
                                    waits=(("cT", mm_done), ("cV", n["cV"])))
                            prev_acc_user[pslot] = ("cV", ev)
                            A_evac.append(ev)
                if gi + 2 < len(groups):
                    load_group(gi + 2, waits=(("cT", n["cT"]),))
                if is_xbc:
                    for fb in range(4):
                        gfb = gi * 4 + fb
                        e = "vector"
                        csem = "cV"
                        a0, a1 = s_acc[0], s_acc[1]
                        w0 = (("cV", n["cV"]), (csem, n[csem]), ("cA", n["cA"]))
                        op(e, lambda eng, fb=fb, gfb=gfb, a0=a0: eng.tensor_scalar(
                            a0[:, :], s_raw[:, fb * (T + 3): fb * (T + 3) + T],
                            s_cw[:, gfb * 4: gfb * 4 + 1], s_cb[:, gfb:gfb + 1],
                            ALU.mult, ALU.add), waits=w0)
                        cur, nxt = a0, a1
                        for w in (1, 2, 3):
                            op(e, lambda eng, fb=fb, gfb=gfb, w=w, cur=cur, nxt=nxt:
                               eng.scalar_tensor_tensor(
                                   nxt[:, :], s_raw[:, fb * (T + 3) + w: fb * (T + 3) + w + T],
                                   s_cw[:, gfb * 4 + w: gfb * 4 + w + 1], cur[:, :],
                                   ALU.mult, ALU.add),
                               waits=((csem, n[csem]),))
                            cur, nxt = nxt, cur
                        conv_done = n[csem]
                        raw_free[fb] = conv_done
                        raw_free_sem[fb] = csem
                        stg = s_stg[2 + fb % 2]
                        su = stg_user.get(2 + fb % 2, ("cV", 0))
                        sv = op("scalar", lambda eng, cur=cur, stg=stg: eng.activation(
                            stg[:, :], cur[:, :], AF.Silu),
                            waits=((csem, conv_done), su, ("cA", n["cA"])))
                        sd = dma("sync", f"dXbc{fb % 2}",
                                 d_xbc[gfb * 128:(gfb + 1) * 128, :], stg[:, :],
                                 waits=(("cA", sv),))
                        stg_user[2 + fb % 2] = (f"dXbc{fb % 2}", sd)
            spills_done = (("dXbc0", nn("dXbc0")), ("dXbc1", nn("dXbc1")),
                           ("dSz0", nn("dSz0")), ("dSz1", nn("dSz1")))

            aw = (("cV", max(A_evac)),)
            op("scalar", lambda eng: eng.activation(s_Atmp[:, :], s_Ahm[:, :], AF.Abs),
               waits=aw + (("cA", n["cA"]),))
            op("scalar", lambda eng: eng.activation(s_Atmp[:, :], s_Atmp[:, :],
                                                    AF.Exp, scale=-1.0), waits=(("cA", n["cA"]),))
            op("scalar", lambda eng: eng.activation(s_Atmp[:, :], s_Atmp[:, :],
                                                    AF.Ln, bias=1.0), waits=(("cA", n["cA"]),))
            sp_done = n["cA"]
            op("vector", lambda eng: eng.tensor_relu(s_Ahm[:, :], s_Ahm[:, :]),
               waits=(("cA", sp_done), ("cV", n["cV"])))
            op("vector", lambda eng: eng.tensor_add(s_Ahm[:, :], s_Ahm[:, :], s_Atmp[:, :]),
               waits=(("cV", n["cV"]),))
            op("vector", lambda eng: eng.tensor_scalar(
                s_Ahm[:, :], s_Ahm[:, :], -1.0, None, ALU.mult), waits=(("cV", n["cV"]),))
            ahm_done = n["cV"]

        if stop_after == "inproj":
            S.barrier("sync"); S.barrier("gpsimd"); S.barrier("scalar")
            S.flush(blk, lambda name: E(nc.semaphore(name)))
            return nc
        with ExitStack() as ph:
            P = ph.enter_context
            s_S = P(nc.sbuf_tensor("s_S", [128, 16 * HD], F32))
            s_Stmp = P(nc.sbuf_tensor("s_Stmp", [128, 16 * HD], F32))
            s_acs = P(nc.sbuf_tensor("s_acs", [NH, 128], F32))
            s_ds1 = P(nc.sbuf_tensor("s_ds1", [NH, 128], F32))
            s_dstm = P(nc.sbuf_tensor("s_dstm", [128, 32], F32))
            s_eahr = P(nc.sbuf_tensor("s_eahr", [128, 16], F32))
            s_eah = P(nc.sbuf_tensor("s_eah", [128, 16], F32))
            s_cum = P(nc.sbuf_tensor("s_cum", [NH, 1], F32))
            s_btm = [P(nc.sbuf_tensor(f"s_btm{i}", [128, D], BF16)) for i in range(2)]
            s_xtm = [P(nc.sbuf_tensor(f"s_xtm{i}", [128, D], BF16)) for i in range(2)]
            s_bd = P(nc.sbuf_tensor("s_bd", [128, D], BF16))
            p_st = P(nc.psum_tensor("p_st", [128, 16 * HD], F32))
            p_tp = P(nc.psum_tensor("p_tp", [128, 32], F32))

            op("vector", lambda eng: eng.memset(s_S[:, :], 0.0),
               waits=(("cV", n["cV"]),))
            op("vector", lambda eng: eng.memset(s_cum[:, :], 0.0),
               waits=(("cV", n["cV"]),))

            btm_ld, xtm_ld = {}, {}
            btm_user, xtm_user = {}, {}
            eah_user = ("cA", 0)
            for c in range(min(2, nch)):
                i = c % 2
                btm_ld[c] = dma("scalar", f"dBtm{i}", s_btm[i][:, :],
                                d_xbc[D:2 * D, c * 128:(c + 1) * 128],
                                waits=spills_done, transpose=True)
                xtm_ld[c] = dma("scalar", f"dXtm{i}", s_xtm[i][:, :],
                                d_xbc[0:D, c * 128:(c + 1) * 128],
                                waits=spills_done, transpose=True)

            for c in range(nch):
                i = c % 2
                sc = op("vector", lambda eng, c=c: eng.tensor_tensor_scan(
                    s_acs[:, :], s_Ahm[:, c * 128:(c + 1) * 128],
                    s_Ahm[:, c * 128:(c + 1) * 128], 0.0, ALU.add, ALU.bypass),
                    waits=(("cV", max(ahm_done, n["cV"])), ("dAcs", nn("dAcs")),
                           ("dAt", nn("dAt")), ("cT", n["cT"]), ("cA", n["cA"])))
                acs_dma = dma("sync", "dAcs", d_acs[c, :, :], s_acs[:, :],
                              waits=(("cV", sc),))
                at_dma = dma("sync", "dAt", d_atot[c, :].unsqueeze(1),
                             s_acs[:, 127:128], waits=(("cV", sc),))
                cum_dma = dma("sync", "dCum", d_cuma[c, :].unsqueeze(1), s_cum[:, :],
                              waits=(("cV", n["cV"]),))
                op("vector", lambda eng: eng.tensor_add(
                    s_cum[:, :], s_cum[:, :], s_acs[:, 127:128]),
                    waits=(("dCum", cum_dma), ("cV", n["cV"])))
                tt = op("tensor", lambda eng: eng.transpose(
                    p_tp[:, 0:32], s_acs[:, :], s_e32f[:, :]),
                    waits=(("cV", sc), ("cT", n["cT"])))
                op("vector", lambda eng, c=c: eng.tensor_copy(
                    s_acstm[:, c * 32:(c + 1) * 32], p_tp[:, 0:32]),
                    waits=(("cT", tt), ("cV", n["cV"])))
                acstm_ev = n["cV"]
                op("vector", lambda eng: eng.tensor_scalar(
                    s_ds1[:, :], s_acs[:, :], s_acs[:, 127:128], None, ALU.subtract),
                    waits=(("cV", n["cV"]), ("dAcs", acs_dma), ("cT", n["cT"])))
                dsv = op("scalar", lambda eng: eng.activation(
                    s_ds1[:, :], s_ds1[:, :], AF.Exp, scale=-1.0),
                    waits=(("cV", n["cV"]), ("cA", n["cA"])))
                tt2 = op("tensor", lambda eng: eng.transpose(
                    p_tp[:, 0:32], s_ds1[:, :], s_e32f[:, :]),
                    waits=(("cA", dsv), ("cV", acstm_ev), ("cT", n["cT"])))
                dst_ev = op("vector", lambda eng: eng.tensor_copy(
                    s_dstm[:, :], p_tp[:, 0:32]), waits=(("cT", tt2), ("cV", n["cV"])))
                bdv = op("vector", lambda eng, i=i: eng.tensor_tensor(
                    s_bd[:, :].rearrange("p (h x) -> p h x", h=NH),
                    s_btm[i][:, :].rearrange("p (h x) -> p h x", h=NH),
                    s_dstm[:, :].unsqueeze(2).broadcast_to([128, NH, HD]), ALU.mult),
                    waits=(("cV", dst_ev), (f"dBtm{i}", btm_ld[c]),
                           ("cV", btm_user.get(i, 0))))
                stw = (("cV", bdv), (f"dXtm{i}", xtm_ld[c]), ("cV", n["cV"]))
                for h2 in range(NH):
                    pb2, hf2 = h2 // 2, h2 % 2
                    op("tensor", lambda eng, h2=h2, i=i, pb2=pb2, hf2=hf2: eng.matmul(
                        p_st[hf2 * 64:hf2 * 64 + 64, pb2 * HD:(pb2 + 1) * HD],
                        s_bd[:, h2 * HD:(h2 + 1) * HD],
                        s_xtm[i][:, h2 * HD:(h2 + 1) * HD], start=True, stop=True),
                        waits=stw if h2 == 0 else (("cT", n["cT"]),))
                st_done = n["cT"]
                btm_user[i] = bdv
                if c + 2 < nch:
                    btm_ld[c + 2] = dma("scalar", f"dBtm{i}", s_btm[i][:, :],
                                        d_xbc[D:2 * D, (c + 2) * 128:(c + 3) * 128],
                                        waits=(("cV", bdv),), transpose=True)
                    xtm_ld[c + 2] = dma("scalar", f"dXtm{i}", s_xtm[i][:, :],
                                        d_xbc[0:D, (c + 2) * 128:(c + 3) * 128],
                                        waits=(("cT", st_done),), transpose=True)
                pd = dma("gpsimd", "dP", d_P[c, :, :], s_S[:, :],
                         waits=(("cV", n["cV"]),))
                dma("scalar", "dEah", s_eahr[0:64, :],
                    d_atot[c, :].rearrange("(pb hf) -> hf pb", hf=2)[0:1, :]
                    .broadcast_to([64, 16]),
                    waits=(("dAt", at_dma), eah_user), allow_slow_non_contiguous=True)
                eb = dma("scalar", "dEah", s_eahr[64:128, :],
                         d_atot[c, :].rearrange("(pb hf) -> hf pb", hf=2)[1:2, :]
                         .broadcast_to([64, 16]),
                         waits=(("dAt", at_dma), eah_user), allow_slow_non_contiguous=True)
                ev = op("scalar", lambda eng: eng.activation(
                    s_eah[:, :], s_eahr[:, :], AF.Exp),
                    waits=(("dEah", eb), ("cA", n["cA"])))
                eah_user = ("cA", ev)
                u1 = op("vector", lambda eng: eng.tensor_tensor(
                    s_Stmp[:, :].rearrange("p (h x) -> p h x", h=16),
                    s_S[:, :].rearrange("p (h x) -> p h x", h=16),
                    s_eah[:, :].unsqueeze(2).broadcast_to([128, 16, HD]), ALU.mult),
                    waits=(("cA", ev), ("dP", pd), ("cV", n["cV"])))
                op("vector", lambda eng: eng.tensor_add(
                    s_S[:, :], s_Stmp[:, :], p_st[:, :]),
                    waits=(("cV", u1), ("cT", st_done)))
            send_dma = dma("sync", "dCc", cc_in, s_S[:, :], waits=(("cV", n["cV"]),))
            cc_tok = S.emit("gpsimd", lambda eng: eng.collective_compute(
                "AllGather", ALU.bypass, replica_groups=[list(range(ncores))],
                ins=[cc_in], outs=[cc_out]), waits=(("dCc", send_dma),), inc="dC", k=1)
            states_end = S.all_state()
        ph123.close()

        if stop_after == "states":
            S.barrier("sync"); S.barrier("gpsimd"); S.barrier("scalar")
            S.flush(blk, lambda name: E(nc.semaphore(name)))
            return nc
        with ExitStack() as ph:
            P = ph.enter_context
            s_xq = P(nc.sbuf_tensor("s_xq", [128, 16 * T], BF16))

            bar = states_end
            w0 = (("cV", bar["cV"]), ("cA", bar["cA"]), ("cT", bar["cT"]))
            qs = ExitStack()
            Q = qs.enter_context
            s_srow = Q(nc.sbuf_tensor("s_srow", [128, T], BF16))
            s_inv = Q(nc.sbuf_tensor("s_inv", [128, T], BF16))
            s_q = Q(nc.sbuf_tensor("s_q", [128, 8], F32))
            s_pk = Q(nc.sbuf_tensor("s_pk", [128, 32], F32))
            p_tp2 = Q(nc.psum_tensor("p_tp2", [32, 128], F32))
            s_rowev = Q(nc.sbuf_tensor("s_rowev", [32, 128], F32))
            ntb = T // 128
            op("vector", lambda eng: eng.memset(s_pk[:, :], 0.0),
               waits=(("cV", n["cV"]),))
            for tb in range(ntb):
                op("vector", lambda eng, tb=tb: eng.tensor_scalar(
                    s_q[:, 0:1], s_var[:, tb:tb + 1], 1e-5, None, ALU.add),
                    waits=w0 + (("cV", n["cV"]),))
                op("vector", lambda eng: eng.reciprocal(s_q[:, 1:2], s_q[:, 0:1]),
                   waits=(("cV", n["cV"]),))
                op("vector", lambda eng, tb=tb: eng.tensor_tensor(
                    s_q[:, 2:3], s_var[:, tb:tb + 1], s_q[:, 1:2], ALU.mult),
                    waits=(("cV", n["cV"]),))
                op("vector", lambda eng: eng.tensor_scalar(
                    s_q[:, 3:4], s_q[:, 2:3], 1e-8, None, ALU.add), waits=(("cV", n["cV"]),))
                op("scalar", lambda eng: eng.activation(
                    s_q[:, 4:5], s_q[:, 3:4], AF.Sqrt), waits=(("cV", n["cV"]), ("cA", n["cA"])))
                op("vector", lambda eng: eng.reciprocal(s_q[:, 5:6], s_q[:, 4:5]),
                   waits=(("cA", n["cA"]), ("cV", n["cV"])))
                op("vector", lambda eng, tb=tb: eng.tensor_tensor(
                    s_q[:, 6:7], s_q[:, 5:6], s_mxn[:, tb:tb + 1], ALU.mult),
                    waits=(("cV", n["cV"]),))
                op("vector", lambda eng, tb=tb: eng.reciprocal(
                    s_q[:, 7:8], s_mxn[:, tb:tb + 1]), waits=(("cV", n["cV"]),))
                op("vector", lambda eng, tb=tb: eng.tensor_scalar(
                    s_pk[:, tb:tb + 1], s_q[:, 7:8], 127.0, None, ALU.mult),
                    waits=(("cV", n["cV"]),))
                op("vector", lambda eng, tb=tb: eng.tensor_scalar(
                    s_pk[:, 16 + tb:17 + tb], s_q[:, 6:7], 1.0 / 127.0, None, ALU.mult),
                    waits=(("cV", n["cV"]),))
            pk_done = n["cV"]
            tt = op("tensor", lambda eng: eng.transpose(
                p_tp2[:, :], s_pk[:, :], s_e128f[:, :]),
                waits=(("cV", pk_done), ("cT", bar["cT"])))
            rv = op("vector", lambda eng: eng.tensor_copy(s_rowev[:, :], p_tp2[:, :]),
                    waits=(("cT", tt), ("cV", n["cV"])))
            r1 = dma("sync", "dSr", d_srow[0:1, :].rearrange("one (tb p) -> (one tb) p", p=128),
                     s_rowev[0:ntb, :], waits=(("cV", rv),))
            r2 = dma("sync", "dSr", d_srow[1:2, :].rearrange("one (tb p) -> (one tb) p", p=128),
                     s_rowev[16:16 + ntb, :], waits=(("cV", rv),))
            b1 = dma("gpsimd", "dSb", s_srow[:, :],
                     d_srow[0:1, :].broadcast_to([128, T]), waits=(("dSr", r2),))
            b2 = dma("gpsimd", "dSb", s_inv[:, :],
                     d_srow[1:2, :].broadcast_to([128, T]), waits=(("dSr", r2),))

            if True:
                s_xsl = [Q(nc.sbuf_tensor(f"s_xsl{i}", [128, T], BF16)) for i in range(2)]
                s_y = [Q(nc.sbuf_tensor(f"s_y{i}", [128, T], F32)) for i in range(2)]
                s_t2 = [Q(nc.sbuf_tensor(f"s_t2{i}", [128, T], F32)) for i in range(2)]
                s_t1 = [Q(nc.sbuf_tensor(f"s_t1{i}", [128, T], F32)) for i in range(2)]
                xsl_user = {}
                for kb in range(16):
                    e = "vector"
                    cs = "cV"
                    i = kb % 2
                    xl = dma("scalar", f"dXsl{i}", s_xsl[i][:, :],
                             d_xn[:, kb * 128:(kb + 1) * 128],
                             waits=(("dXnSp", ln_sp), ("dSb", max(b1, b2)),
                                    xsl_user.get(i, ("cV", 0))),
                             transpose=True)
                    op(e, lambda eng, i=i: eng.tensor_tensor(
                        s_y[i][:, :], s_xsl[i][:, :], s_srow[:, :], ALU.mult),
                        waits=((f"dXsl{i}", xl), ("cV", flag_done), (cs, n[cs])))
                    op(e, lambda eng, i=i: eng.tensor_scalar(
                        s_t2[i][:, :], s_y[i][:, :], C23, -C23, ALU.add, ALU.add),
                        waits=((cs, n[cs]),))
                    op(e, lambda eng, i=i: eng.tensor_scalar(
                        s_t1[i][:, :], s_y[i][:, :], C52, -C52, ALU.add, ALU.add),
                        waits=((cs, n[cs]),))
                    op(e, lambda eng, i=i: eng.tensor_sub(
                        s_t1[i][:, :], s_t1[i][:, :], s_t2[i][:, :]),
                        waits=((cs, n[cs]),))
                    op(e, lambda eng, i=i: eng.scalar_tensor_tensor(
                        s_t2[i][:, :], s_t1[i][:, :], s_flag[:, 0:1], s_t2[i][:, :],
                        ALU.mult, ALU.add), waits=((cs, n[cs]),))
                    fin = op(e, lambda eng, i=i, kb=kb: eng.scalar_tensor_tensor(
                        s_xq[:, kb * T:(kb + 1) * T], s_t2[i][:, :], 127.0, s_inv[:, :],
                        ALU.min, ALU.mult), waits=((cs, n[cs]),))
                    xsl_user[i] = (cs, fin)
                xq_cV, xq_cP = n["cV"], n["cP"]
                if "d_xq" in debug:
                    for kb in range(16):
                        dma("sync", "dDbgQ", d_xq[kb * 128:(kb + 1) * 128, :],
                            s_xq[:, kb * T:(kb + 1) * T],
                            waits=(("cV", xq_cV), ("cP", xq_cP)))
            qs.close()

            s_wg = [P(nc.sbuf_tensor(f"s_wg{i}", [128, 16 * 256], BF16)) for i in range(2)]
            s_wu = [P(nc.sbuf_tensor(f"s_wu{i}", [128, 16 * 256], BF16)) for i in range(2)]
            s_wd = P(nc.sbuf_tensor("s_wd", [128, 4 * 2048], BF16))
            s_fa = P(nc.sbuf_tensor("s_fa", [128, 16 * TH], BF16))
            s_psb = [P(nc.sbuf_tensor(f"s_psb{i}", [128, 2 * TH], BF16)) for i in range(2)]
            s_part = P(nc.sbuf_tensor("s_part", [128, 16 * 1024], BF16))
            s_gs = P(nc.sbuf_tensor("s_gs", [128, 512], BF16))
            s_fst = [P(nc.sbuf_tensor(f"s_fst{i}", [128, TH], BF16)) for i in range(2)]
            p_up = [P(nc.psum_tensor(f"p_up{i}", [128, 512], F32)) for i in range(4)]
            p_dn = [P(nc.psum_tensor(f"p_dn{i}", [128, 512], F32)) for i in range(4)]

            wgv = {0: ("cA", 0), 1: ("cA", 0)}
            wuv = {0: ("cA", 0), 1: ("cA", 0)}
            wdv = ("cA", 0)
            pup_user, puu_user, pdn_user = {}, {}, {}
            wg_user = {0: ("cT", 0), 1: ("cT", 0)}
            wu_user = {0: ("cT", 0), 1: ("cT", 0)}
            wd_user = {0: ("cT", 0), 1: ("cT", 0)}
            part_user = {}
            psb_user = {0: ("cT", 0), 1: ("cT", 0)}
            fa_user = ("cA", 0)
            gs_user = {}
            for hh in range(2):
                toff = hh * TH
                tq_sp = spans(0, TH, 512)
                op("vector", lambda eng: eng.memset(s_fa[:, :], 0.0),
                   waits=(("cV", n["cV"]), fa_user, ("dDbgF", nn("dDbgF")),
                          ("dAc0", nn("dAc0")), ("dAc1", nn("dAc1"))))
                fa_zero = n["cV"]
                for g in range(32):
                    sl = g % 2
                    lg = dma("gpsimd", f"dWg{sl}",
                             s_wg[sl][:, :].rearrange("p (ks m) -> p ks m", ks=16),
                             t_wgT.rearrange("(ks kp) m -> kp ks m", kp=128)[:, :, g * 256:(g + 1) * 256],
                             waits=(wg_user[sl], ("cV", xq_cV), ("cP", xq_cP),
                                    ("dDbgW", nn("dDbgW"))))
                    lu = dma("gpsimd", f"dWu{sl}",
                             s_wu[sl][:, :].rearrange("p (ks m) -> p ks m", ks=16),
                             t_wuT.rearrange("(ks kp) m -> kp ks m", kp=128)[:, :, g * 256:(g + 1) * 256],
                             waits=(wu_user[sl], ("cV", xq_cV), ("cP", xq_cP)))
                    ld_ = dma("gpsimd", f"dWd{sl}", s_wd[:, sl * 2 * 2048:(sl + 1) * 2 * 2048].rearrange(
                        "p (ks m) -> p ks m", ks=2),
                        t_wdT.rearrange("(ks kp) m -> kp ks m", kp=128)[:, g * 2:(g + 1) * 2, :],
                        waits=(wd_user[sl], ("cV", xq_cV), ("cP", xq_cP)))
                    wgv[sl] = ("cA", op("scalar", lambda eng, sl=sl: eng.activation(
                        s_wg[sl][:, :], s_wg[sl][:, :], AF.Sign),
                        waits=((f"dWg{sl}", lg), ("cA", n["cA"]))))
                    wuv[sl] = ("cA", op("scalar", lambda eng, sl=sl: eng.activation(
                        s_wu[sl][:, :], s_wu[sl][:, :], AF.Sign),
                        waits=((f"dWu{sl}", lu), ("cA", n["cA"]))))
                    wdv = ("cA", op("scalar", lambda eng, sl=sl: eng.activation(
                        s_wd[:, sl * 2 * 2048:(sl + 1) * 2 * 2048],
                        s_wd[:, sl * 2 * 2048:(sl + 1) * 2 * 2048], AF.Sign),
                        waits=((f"dWd{sl}", ld_), ("cA", n["cA"]))))
                    for mb in range(2):
                        for ti, (q0, q1) in enumerate(tq_sp):
                            nw = q1 - q0
                            idx = (mb * len(tq_sp) + ti) % 2
                            pg = p_up[idx]
                            pu = p_up[2 + idx]
                            for ks in range(16):
                                wts = (wgv[sl], ("cV", xq_cV), ("cP", xq_cP),
                                       pup_user.get(idx, ("cA", 0))) if ks == 0 else ()
                                op("tensor", lambda eng, pg=pg, ks=ks, mb=mb, q0=q0, nw=nw, toff=toff, sl=sl:
                                   eng.matmul(pg[:, 0:nw],
                                              s_wg[sl][:, :].rearrange("p (ks m) -> p ks m", ks=16)[:, ks, mb * 128:(mb + 1) * 128],
                                              s_xq[:, ks * T + toff + q0: ks * T + toff + q0 + nw],
                                              start=(ks == 0), stop=(ks == 15)), waits=wts)
                            gmm = n["cT"]
                            gv = op("scalar", lambda eng, pg=pg, nw=nw: eng.activation(
                                s_gs[:, 0:nw], pg[:, 0:nw], AF.Silu),
                                waits=(("cT", gmm), ("cA", n["cA"]), ("cV", n["cV"])))
                            pup_user[idx] = ("cA", gv)
                            for ks in range(16):
                                wts = (wuv[sl], ("cA", gv),
                                       puu_user.get(idx, ("cV", 0))) if ks == 0 else ()
                                op("tensor", lambda eng, pu=pu, ks=ks, mb=mb, q0=q0, nw=nw, toff=toff, sl=sl:
                                   eng.matmul(pu[:, 0:nw],
                                              s_wu[sl][:, :].rearrange("p (ks m) -> p ks m", ks=16)[:, ks, mb * 128:(mb + 1) * 128],
                                              s_xq[:, ks * T + toff + q0: ks * T + toff + q0 + nw],
                                              start=(ks == 0), stop=(ks == 15)), waits=wts)
                            umm = n["cT"]
                            pv = op("vector", lambda eng, pu=pu, sl=sl, mb=mb, q0=q0, nw=nw:
                                    eng.tensor_tensor(
                                        s_psb[sl][:, mb * TH + q0: mb * TH + q0 + nw],
                                        pu[:, 0:nw], s_gs[:, 0:nw], ALU.mult),
                                    waits=(("cT", umm), ("cA", gv), psb_user[sl],
                                           ("cV", n["cV"])))
                            puu_user[idx] = ("cV", pv)
                    psb_done = n["cV"]
                    wg_user[sl] = ("cT", n["cT"])
                    wu_user[sl] = ("cT", n["cT"])
                    if g % 2 == 1:
                        for ti, (q0, q1) in enumerate(tq_sp):
                            nw = q1 - q0
                            for db in range(16):
                                pd_ = p_dn[db % 4]
                                for kk in range(4):
                                    wts = (wdv, ("cV", psb_done),
                                           part_user.get(ti, (f"dAc{ti}", 0)),
                                           pdn_user.get(db % 4, ("cV", 0))) if kk == 0 else ()
                                    op("tensor", lambda eng, pd_=pd_, kk=kk, db=db, q0=q0, nw=nw:
                                       eng.matmul(pd_[:, 0:nw],
                                                  s_wd[:, :].rearrange("p (ks m) -> p ks m", ks=4)[:, kk, db * 128:(db + 1) * 128],
                                                  s_psb[kk // 2][:, (kk % 2) * TH + q0: (kk % 2) * TH + q0 + nw],
                                                  start=(kk == 0), stop=(kk == 3)), waits=wts + ((("dDbgP", nn("dDbgP")),) if kk == 0 else ()))
                                dmm = n["cT"]
                                pdn_user[db % 4] = ("cV", op(
                                    "vector", lambda eng, pd_=pd_, db=db, nw=nw, ti=ti:
                                    eng.tensor_copy(
                                        s_part[:, db * 1024 + ti * 512: db * 1024 + ti * 512 + nw],
                                        pd_[:, 0:nw]),
                                    waits=(("cT", dmm), ("cV", n["cV"]))))
                            evd = n["cV"]
                            ac = dma("gpsimd", f"dAc{ti}", s_fa[:, :].rearrange(
                                "p (db t) -> p db t", db=16)[:, :, q0:q1],
                                s_part[:, :].rearrange("p (db t) -> p db t", db=16)[:, :, ti * 512:ti * 512 + nw],
                                waits=(("cV", evd), ("cV", fa_zero)), accum_op=ALU.add)
                            part_user[ti] = (f"dAc{ti}", ac)
                        wd_user[0] = ("cT", n["cT"])
                        wd_user[1] = ("cT", n["cT"])
                        psb_user[0] = ("cT", n["cT"])
                        psb_user[1] = ("cT", n["cT"])
                accw = tuple(part_user.get(ti, (f"dAc{ti}", 0)) for ti in range(len(tq_sp)))
                if "d_f" in debug:
                    for db in range(16):
                        dma("sync", "dDbgF", d_f[db * 128:(db + 1) * 128, toff:toff + TH],
                            s_fa[:, db * TH:(db + 1) * TH], waits=accw)
                for db in range(16):
                    i = db % 2
                    su = gs_user.get(i)
                    sv = op("scalar", lambda eng, db=db, i=i: eng.activation(
                        s_fst[i][:, :], s_fa[:, db * TH:(db + 1) * TH], AF.Silu),
                        waits=accw + (("cA", n["cA"]),) + ((su,) if su else ()))
                    gs_user[i] = (f"dGs{i}", dma(
                        "sync", f"dGs{i}", d_g[db * 128:(db + 1) * 128, toff:toff + TH],
                        s_fst[i][:, :], waits=(("cA", sv),)))
                fa_user = ("cA", n["cA"])
            ffn_end = S.all_state()

        if stop_after == "ffn":
            S.barrier("sync"); S.barrier("gpsimd"); S.barrier("scalar")
            S.flush(blk, lambda name: E(nc.semaphore(name)))
            return nc
        with ExitStack() as ph:
            P = ph.enter_context
            s_term = [P(nc.sbuf_tensor(f"s_term{i}", [128, NH * 128], F32)) for i in range(2)]
            s_nbig = P(nc.sbuf_tensor("s_nbig", [128, NH * 128], F32))
            s_LT = P(nc.sbuf_tensor("s_LT", [128, NH * 128], BF16))
            s_MT = P(nc.sbuf_tensor("s_MT", [128, NH * 128], BF16))
            s_ea2 = P(nc.sbuf_tensor("s_ea2", [128, D], BF16))
            s_ctil = P(nc.sbuf_tensor("s_ctil", [128, D], BF16))
            s_acsb = [P(nc.sbuf_tensor(f"s_acsb{i}", [NH, 128], BF16)) for i in range(2)]
            s_Bc = [P(nc.sbuf_tensor(f"s_Bc{i}", [128, 16 * 128], BF16)) for i in range(2)]
            s_Cc = [P(nc.sbuf_tensor(f"s_Cc{i}", [128, 16 * 128], BF16)) for i in range(2)]
            s_szc = [P(nc.sbuf_tensor(f"s_szc{i}", [128, 16 * 128], BF16)) for i in range(2)]
            s_xt2 = [P(nc.sbuf_tensor(f"s_xt2{i}", [128, D], BF16)) for i in range(2)]
            s_Pc = [P(nc.sbuf_tensor(f"s_Pc{i}", [128, 16 * HD], BF16)) for i in range(2)]
            s_Sg = P(nc.sbuf_tensor("s_Sg", [128, 16 * HD], F32))
            s_Sgt = P(nc.sbuf_tensor("s_Sgt", [128, 16 * HD], F32))
            s_Sg2 = P(nc.sbuf_tensor("s_Sg2", [128, 16 * HD], F32))
            s_sin = [P(nc.sbuf_tensor(f"s_sin{i}", [128, 16 * HD], BF16)) for i in range(2)]
            s_ecr = [P(nc.sbuf_tensor(f"s_ecr{i}", [128, 16], F32)) for i in range(2)]
            s_yg = P(nc.sbuf_tensor("s_yg", [128, D], BF16))
            s_sq2 = P(nc.sbuf_tensor("s_sq2", [128, D], BF16))
            s_yn = P(nc.sbuf_tensor("s_yn", [128, D], BF16))
            s_r1 = P(nc.sbuf_tensor("s_r1", [1, 128], F32))
            s_rb = P(nc.sbuf_tensor("s_rb", [128, D], F32))
            p_eg = P(nc.psum_tensor("p_eg", [128, 2048], F32))
            p_y = P(nc.psum_tensor("p_y", [128, 2048], F32))

            s_bo = P(nc.sbuf_tensor("s_bo", [NH, D], BF16))
            s_DI = P(nc.sbuf_tensor("s_DI", [128, NH * 128], BF16))

            bar = ffn_end
            w0 = barw(bar)
            nb_ld = dma("sync", "dNb", s_nbig[:, :], t_nbig, waits=w0)
            bo_ld = dma("sync", "dBo", s_bo[:, :], t_bo, waits=w0)
            for h in range(NH):
                op("vector", lambda eng, h=h: eng.tensor_scalar(
                    s_DI[:, h * 128:(h + 1) * 128], s_e128b[:, :], s_drep[:, h:h + 1],
                    None, ALU.mult),
                    waits=w0 + (("dCst", consts_dS), ("dCst2", consts_dA), ("cV", n["cV"])))
            di_done = n["cV"]
            sgv = 0
            for r in range(ncores):
                sld = dma("sync", "dSg", s_Sgt[:, :],
                          cc_out.rearrange("(r p) m -> r p m", p=128)[r, :, :],
                          waits=w0 + (("dC", 1), ("cV", sgv)))
                if r == 0:
                    sgv = op("vector", lambda eng: eng.tensor_scalar(
                        s_Sg[:, :], s_Sgt[:, :], s_sel[:, 0:1], None, ALU.mult),
                        waits=(("dSg", sld), ("cV", n["cV"])))
                else:
                    sgv = op("vector", lambda eng, r=r: eng.scalar_tensor_tensor(
                        s_Sg[:, :], s_Sgt[:, :], s_sel[:, r:r + 1], s_Sg[:, :],
                        ALU.mult, ALU.add), waits=(("dSg", sld), ("cV", n["cV"])))

            per_chunk = {}
            ld_user = {}

            def chunk_loads(c):
                i = c % 2
                u = ld_user.get(i, {})
                per_chunk[c] = dict(
                    b=dma("sync", f"dYb{i}", s_Bc[i][:, :].rearrange("p (fb t) -> p fb t", fb=16),
                          d_xbc.rearrange("(fb p) t -> p fb t", p=128)[:, 16:32, c * 128:(c + 1) * 128],
                          waits=w0 + (u.get("b", ("cT", 0)),)),
                    cc=dma("sync", f"dYc{i}", s_Cc[i][:, :].rearrange("p (fb t) -> p fb t", fb=16),
                           d_xbc.rearrange("(fb p) t -> p fb t", p=128)[:, 32:48, c * 128:(c + 1) * 128],
                           waits=w0 + (u.get("cc", ("cT", 0)),)),
                    sz=dma("sync", f"dYs{i}", s_szc[i][:, :].rearrange("p (fb t) -> p fb t", fb=16),
                           d_sz.rearrange("(fb p) t -> p fb t", p=128)[:, :, c * 128:(c + 1) * 128],
                           waits=w0 + (u.get("sz", ("cV", 0)),)),
                    x=dma("scalar", f"dYx{i}", s_xt2[i][:, :], d_xbc[0:D, c * 128:(c + 1) * 128],
                          waits=w0 + (u.get("x", ("cT", 0)),), transpose=True),
                    p=dma("sync", f"dYp{i}", s_Pc[i][:, :], d_P[c, :, :],
                          waits=w0 + (u.get("p", ("cV", 0)),)),
                    term=dma("scalar", f"dYt{i}", s_term[i][:, :],
                             d_acs[c, :, :].rearrange("h t -> (h t)").unsqueeze(0)
                             .broadcast_to([128, NH * 128]),
                             waits=w0 + (u.get("term", ("cA", 0)),)),
                    acs=dma("gpsimd", f"dYa{i}", s_acsb[i][:, :], d_acs[c, :, :],
                            waits=w0 + (u.get("acs", ("cT", 0)),)),
                    ec0=dma("scalar", f"dYe{i}", s_ecr[i][0:64, :],
                            d_cuma[c, :].rearrange("(pb hf) -> hf pb", hf=2)[0:1, :]
                            .broadcast_to([64, 16]),
                            waits=w0 + (u.get("ec", ("cV", 0)),), allow_slow_non_contiguous=True),
                    ec=dma("scalar", f"dYe{i}", s_ecr[i][64:128, :],
                           d_cuma[c, :].rearrange("(pb hf) -> hf pb", hf=2)[1:2, :]
                           .broadcast_to([64, 16]),
                           waits=w0 + (u.get("ec", ("cV", 0)),), allow_slow_non_contiguous=True),
                )
                return per_chunk[c]

            chunk_loads(0)
            yn_user, rd_user = 0, 0

            for c in range(nch if y_chunks is None else min(y_chunks, nch)):
                i = c % 2
                L = per_chunk[c]
                if y_sub < 1:
                    continue
                e1 = op("scalar", lambda eng, i=i: eng.activation(
                    s_ecr[i][:, :], s_ecr[i][:, :], AF.Exp),
                    waits=((f"dYe{i}", L["ec"]), ("cA", n["cA"])))
                u1 = op("vector", lambda eng, i=i: eng.tensor_tensor(
                    s_Sg2[:, :].rearrange("p (h x) -> p h x", h=16),
                    s_Sg[:, :].rearrange("p (h x) -> p h x", h=16),
                    s_ecr[i][:, :].unsqueeze(2).broadcast_to([128, 16, HD]), ALU.mult),
                    waits=(("cA", e1), ("cV", max(sgv, n["cV"]))))
                sv = op("vector", lambda eng, i=i: eng.tensor_add(
                    s_sin[i][:, :], s_Sg2[:, :], s_Pc[i][:, :]),
                    waits=(("cV", u1), (f"dYp{i}", L["p"])))
                if y_sub < 2:
                    continue
                t1 = op("vector", lambda eng, i=i, c=c: eng.tensor_tensor(
                    s_term[i][:, :].rearrange("p (h t) -> p h t", h=NH),
                    s_term[i][:, :].rearrange("p (h t) -> p h t", h=NH),
                    s_acstm[:, c * 32:(c + 1) * 32].unsqueeze(2).broadcast_to([128, NH, 128]),
                    ALU.subtract), waits=((f"dYt{i}", L["term"]), ("cV", n["cV"])))
                t2 = op("vector", lambda eng, i=i: eng.scalar_tensor_tensor(
                    s_term[i][:, :], s_term[i][:, :], 87.0, s_nbig[:, :], ALU.min, ALU.add),
                    waits=(("cV", t1), ("dNb", nb_ld)))
                lt = op("scalar", lambda eng, i=i: eng.activation(
                    s_LT[:, :], s_term[i][:, :], AF.Exp), waits=(("cV", t2), ("cA", n["cA"])))
                if y_sub < 3:
                    continue
                for pb in range(16):
                    wts = ((f"dYa{i}", L["acs"]), ("cV", n["cV"]), ("cA", n["cA"]),
                           ("dBo", bo_ld), ("cT", n["cT"])) if pb == 0 else ()
                    op("tensor", lambda eng, pb=pb, i=i: eng.matmul(
                        p_eg[:, pb * 128:(pb + 1) * 128],
                        s_bo[:, pb * 128:(pb + 1) * 128], s_acsb[i][:, :],
                        start=True, stop=True), waits=wts)
                ea_mm = n["cT"]
                eav = op("scalar", lambda eng: eng.activation(
                    s_ea2[:, :], p_eg[:, :], AF.Exp), waits=(("cT", ea_mm), ("cA", n["cA"])))
                ctv = op("vector", lambda eng, i=i: eng.tensor_mul(
                    s_ctil[:, :], s_Cc[i][:, :], s_ea2[:, :]),
                    waits=(("cA", eav), ((f"dYc{i}"), L["cc"]), ("cV", n["cV"])))
                if y_sub < 4:
                    continue
                mvals = []
                for grp in range(2):
                    first = True
                    for half in range(2):
                        for pbi in range(8):
                            pb = grp * 8 + pbi
                            sl = half * 8 + pbi
                            wts = ((f"dYb{i}", L["b"]), ("cA", eav), ("cV", n["cV"]),
                                   ("cT", n["cT"])) if first else ()
                            first = False
                            op("tensor", lambda eng, i=i, pb=pb, half=half, sl=sl:
                               eng.matmul(
                                   p_eg[:, sl * 128:(sl + 1) * 128],
                                   s_Bc[i][half * 64:half * 64 + 64, pb * 128:(pb + 1) * 128],
                                   s_Cc[i][half * 64:half * 64 + 64, pb * 128:(pb + 1) * 128],
                                   start=True, stop=True), waits=wts)
                    gmm = n["cT"]
                    if y_sub >= 45:
                        for half in range(2):
                            mv = op("vector", lambda eng, grp=grp, half=half: eng.tensor_mul(
                                s_MT[:, grp * 2048:(grp + 1) * 2048].rearrange(
                                    "p (pb half t) -> p half pb t", pb=8, half=2)[:, half],
                                p_eg[:, half * 1024:(half + 1) * 1024].rearrange(
                                    "p (pb t) -> p pb t", pb=8),
                                s_LT[:, grp * 2048:(grp + 1) * 2048].rearrange(
                                    "p (pb half t) -> p half pb t", pb=8, half=2)[:, half]),
                                waits=(("cT", gmm), ("cA", lt), ("cV", n["cV"])))
                            mvals.append(mv)
                    else:
                        mvals.append(0)
                if y_sub < 5:
                    continue
                for h in range(NH):
                    pb, half = h // 2, h % 2
                    wts = ()
                    if h == 0:
                        wts = (("cV", max(mvals)), ("cV", ctv), ("cV", sv),
                               (f"dYx{i}", L["x"]), ("cV", di_done), ("cT", n["cT"]))
                    op("tensor", lambda eng, i=i, h=h, pb=pb, half=half: eng.matmul(
                        p_y[half * 64:half * 64 + 64, pb * 128:(pb + 1) * 128],
                        s_xt2[i][:, h * 64:(h + 1) * 64],
                        s_MT[:, h * 128:(h + 1) * 128], start=True, stop=False),
                        waits=wts)
                    op("tensor", lambda eng, i=i, h=h, pb=pb, half=half: eng.matmul(
                        p_y[half * 64:half * 64 + 64, pb * 128:(pb + 1) * 128],
                        s_xt2[i][:, h * 64:(h + 1) * 64],
                        s_DI[:, h * 128:(h + 1) * 128], start=False, stop=False))
                    op("tensor", lambda eng, i=i, h=h, pb=pb, half=half: eng.matmul(
                        p_y[half * 64:half * 64 + 64, pb * 128:(pb + 1) * 128],
                        s_sin[i][half * 64:half * 64 + 64, pb * HD:(pb + 1) * HD],
                        s_ctil[:, :].rearrange("p (fb t) -> p fb t", fb=16)[half * 64:half * 64 + 64, pb, :],
                        start=False, stop=True))
                y_mm = n["cT"]
                if y_sub < 6:
                    continue
                ygv = op("vector", lambda eng, i=i: eng.tensor_mul(
                    s_yg[:, :], p_y[:, :], s_szc[i][:, :]),
                    waits=(("cT", y_mm), ((f"dYs{i}"), L["sz"]), ("cV", n["cV"])))
                sqv = op("scalar", lambda eng: eng.activation(
                    s_sq2[:, :], s_yg[:, :], AF.Square), waits=(("cV", ygv), ("cA", n["cA"])))
                for pb in range(16):
                    wts = (("cA", sqv), ("cV", max(mvals)), ("cT", n["cT"])) if pb == 0 else ()
                    op("tensor", lambda eng, pb=pb: eng.matmul(
                        p_eg[0:1, 0:128], s_ones[:, :], s_sq2[:, pb * 128:(pb + 1) * 128],
                        start=(pb == 0), stop=(pb == 15)), waits=wts)
                stat_mm = n["cT"]
                rv1 = op("vector", lambda eng: eng.tensor_scalar(
                    s_r1[:, :], p_eg[0:1, 0:128], 1.0 / D, 1e-5, ALU.mult, ALU.add),
                    waits=(("cT", stat_mm), ("cV", n["cV"]), ("dR", rd_user)))
                rs = op("scalar", lambda eng: eng.activation(
                    s_r1[:, :], s_r1[:, :], AF.Sqrt), waits=(("cV", rv1), ("cA", n["cA"])))
                rv2 = op("vector", lambda eng: eng.reciprocal(
                    s_r1[:, :], s_r1[:, :]), waits=(("cA", rs), ("cV", n["cV"])))
                rd = dma("sync", "dR", d_rstd[c * 128:(c + 1) * 128].unsqueeze(0),
                         s_r1[:, :], waits=(("cV", rv2),))
                rd_user = rd
                rb = dma("scalar", "dR", s_rb[:, :].rearrange("p (fb t) -> p fb t", fb=16),
                         d_rstd[c * 128:(c + 1) * 128].unsqueeze(0).unsqueeze(0)
                         .broadcast_to([128, 16, 128]), waits=(("dR", rd),))
                ynv = op("vector", lambda eng: eng.tensor_mul(
                    s_yn[:, :], s_yg[:, :], s_rb[:, :]),
                    waits=(("dR", rb), ("cV", n["cV"]), ("dYn", yn_user)))
                yn_user = nn("dYn") + 16
                dma("sync", "dYn",
                    d_yn.rearrange("(fb p) t -> p fb t", p=128)[:, :, c * 128:(c + 1) * 128],
                    s_yn[:, :].rearrange("p (fb t) -> p fb t", fb=16), waits=(("cV", ynv),))
                ld_user[i] = dict(b=("cT", y_mm), cc=("cV", ctv), sz=("cV", ygv),
                                  x=("cT", y_mm), p=("cV", sv), term=("cA", lt),
                                  acs=("cT", ea_mm), ec=("cV", u1))
                if c + 1 < nch:
                    chunk_loads(c + 1)
            ypass_end = S.all_state()

        if stop_after == "ypass":
            S.barrier("sync"); S.barrier("gpsimd"); S.barrier("scalar")
            S.flush(blk, lambda name: E(nc.semaphore(name)))
            return nc
        with ExitStack() as ph:
            P = ph.enter_context
            s_wo = P(nc.sbuf_tensor("s_wo", [128, 16 * D], BF16))
            s_ynl = P(nc.sbuf_tensor("s_ynl", [128, 16 * T], BF16))
            s_gsl = [P(nc.sbuf_tensor(f"s_gsl{i}", [128, T], BF16)) for i in range(2)]
            s_tst = [P(nc.sbuf_tensor(f"s_tst{i}", [128, T], BF16)) for i in range(2)]
            s_xf = [P(nc.sbuf_tensor(f"s_xf{i}", [128, D], F32)) for i in range(2)]
            s_tt = [P(nc.sbuf_tensor(f"s_tt{i}", [128, D], BF16)) for i in range(2)]
            s_of = [P(nc.sbuf_tensor(f"s_of{i}", [128, D], F32)) for i in range(2)]
            p_o = [P(nc.psum_tensor(f"p_o{i}", [128, 512], F32)) for i in range(2)]

            bar = ypass_end
            w0 = barw(bar)
            wo_ld = dma("gpsimd", "dWo", s_wo[:, :].rearrange("p (ks m) -> p ks m", ks=16),
                        t_woutT.rearrange("(ks kp) m -> kp ks m", kp=128), waits=w0)
            rmv = []
            for ks in range(16):
                rmv.append(op("vector", lambda eng, ks=ks: eng.tensor_scalar(
                    s_wo[:, ks * D:(ks + 1) * D], s_wo[:, ks * D:(ks + 1) * D],
                    s_rw[:, ks:ks + 1], None, ALU.mult),
                    waits=(("dWo", wo_ld), ("cV", n["cV"])) + w0))
            wo_done = n["cV"]
            yn_ld = []
            for ks in range(16):
                yn_ld.append(dma("sync", "dYnRd", s_ynl[:, ks * T:(ks + 1) * T],
                                 d_yn[ks * 128:(ks + 1) * 128, :], waits=w0))
            yn_dS = nn("dYnRd")
            g_ld = {}
            g_user = {}
            for db in range(min(2, 16)):
                g_ld[db] = dma("sync", f"dGld{db % 2}", s_gsl[db % 2][:, :],
                               d_g[db * 128:(db + 1) * 128, :], waits=w0)
            tspill = []
            tst_user = {}
            for db in range(16):
                i = db % 2
                for ti, (q0, q1) in enumerate(spans(0, T, 512)):
                    nw = q1 - q0
                    po = p_o[ti % 2]
                    for ks in range(16):
                        wts = (("cV", wo_done), ("dYnRd", yn_dS),
                               ("cV", n["cV"])) if ks == 0 else ()
                        op("tensor", lambda eng, po=po, ks=ks, db=db, q0=q0, q1=q1, nw=nw:
                           eng.matmul(po[:, 0:nw],
                                      s_wo[:, :].rearrange("p (ks m) -> p ks m", ks=16)[:, ks, db * 128:(db + 1) * 128],
                                      s_ynl[:, :].rearrange("p (ks t) -> p ks t", ks=16)[:, ks, q0:q1],
                                      start=(ks == 0), stop=(ks == 15)), waits=wts)
                    omm = n["cT"]
                    op("vector", lambda eng, po=po, i=i, q0=q0, nw=nw, db=db:
                       eng.tensor_mul(s_tst[i][:, q0:q0 + nw], po[:, 0:nw],
                                      s_gsl[i][:, q0:q0 + nw]),
                       waits=(("cT", omm), ((f"dGld{db % 2}"), g_ld[db]),
                              tst_user.get(i, ("cV", 0)), ("cV", n["cV"])))
                tv = n["cV"]
                td = dma("sync", f"dTs{i}", d_t[db * 128:(db + 1) * 128, :], s_tst[i][:, :],
                         waits=(("cV", tv),))
                tspill.append(td)
                tst_user[i] = (f"dTs{i}", td)
                if db + 2 < 16:
                    g_ld[db + 2] = dma("sync", f"dGld{i}", s_gsl[i][:, :],
                                       d_g[(db + 2) * 128:(db + 3) * 128, :],
                                       waits=(("cV", tv),))
            t_done = (("dTs0", nn("dTs0")), ("dTs1", nn("dTs1")))
            of_user = {}
            out_dmas = []
            for tb in range(T // 128):
                i = tb % 2
                xl = dma("sync", f"dXf{i}", s_xf[i][:, :], t_x[tb * 128:(tb + 1) * 128, :],
                         waits=(of_user.get(i, ("cV", 0)),))
                tl = dma("scalar", f"dTt{i}", s_tt[i][:, :], d_t[:, tb * 128:(tb + 1) * 128],
                         waits=t_done + (of_user.get(i, ("cV", 0)),), transpose=True)
                ov = op("vector", lambda eng, i=i: eng.tensor_add(
                    s_of[i][:, :], s_xf[i][:, :], s_tt[i][:, :]),
                    waits=((f"dXf{i}", xl), (f"dTt{i}", tl), ("cV", n["cV"])))
                od = dma("sync", f"dOut{i}", t_out[tb * 128:(tb + 1) * 128, :], s_of[i][:, :],
                         waits=(("cV", ov),))
                out_dmas.append(od)
                of_user[i] = (f"dOut{i}", od)
            S.barrier("sync")
            S.barrier("gpsimd")

        S.flush(blk, lambda name: E(nc.semaphore(name)))
    return nc



def _consts():
    negbig = np.zeros((128, NH * 128), np.float32)
    sidx = np.arange(128)[:, None]
    tidx = np.arange(128)[None, :]
    m = np.where(sidx > tidx, np.float32(NEGBIG), np.float32(0.0))
    negbig = np.tile(m, (1, NH)).astype(np.float32)
    bo = np.zeros((NH, D), ml_dtypes.bfloat16)
    for pb in range(16):
        for q in range(128):
            bo[2 * pb + q // 64, pb * 128 + q] = 1
    return dict(
        negbig=np.ascontiguousarray(negbig),
        blockones=np.ascontiguousarray(bo),
        ones128b=np.ones((128, 1), ml_dtypes.bfloat16),
        eye128b=np.eye(128, dtype=ml_dtypes.bfloat16),
        eye128f=np.eye(128, dtype=np.float32),
        eye32f=np.eye(32, dtype=np.float32),
        eye3b=np.eye(3, dtype=ml_dtypes.bfloat16),
    )


def make_inmaps(ins, T=2048, ncores=NCORE):
    U = np.asarray(ins["L0_attn_u"], np.float32)
    consts = _consts()
    shared = dict(
        winT=np.ascontiguousarray(np.asarray(ins["L0_attn_W_in"], np.float32).T),
        woutT=np.ascontiguousarray(np.asarray(ins["L0_attn_W_out"], np.float32).T),
        wgT=np.ascontiguousarray(np.asarray(ins["L0_ffn_W_g"], np.float32).T),
        wuT=np.ascontiguousarray(np.asarray(ins["L0_ffn_W_u"], np.float32).T),
        wdT=np.ascontiguousarray(np.asarray(ins["L0_ffn_W_d"], np.float32).T),
        zbias_pm=np.ascontiguousarray(
            np.asarray(ins["L0_attn_z_bias"], np.float32).reshape(16, 128).T),
        convw_pm=np.ascontiguousarray(
            np.asarray(ins["L0_attn_conv_w"], np.float32).reshape(48, 128, 4)
            .transpose(1, 0, 2).reshape(128, 192)),
        convb_pm=np.ascontiguousarray(
            np.asarray(ins["L0_attn_conv_b"], np.float32).reshape(48, 128).T),
        rmsw_pm=np.ascontiguousarray(
            np.asarray(ins["L0_attn_rms_w"], np.float32).reshape(16, 128).T),
        dvec=np.ascontiguousarray(np.asarray(ins["L0_attn_D"], np.float32)),
        **consts,
    )
    in_maps = []
    for core in range(ncores):
        b, h = core // 2, core % 2
        x = np.ascontiguousarray(U[b, h * T:(h + 1) * T, :])
        halo = (np.zeros((3, D), np.float32) if h == 0
                else np.ascontiguousarray(U[b, h * T - 3:h * T, :]))
        sel = np.zeros((128, ncores), np.float32)
        if h == 1:
            sel[:, (core - 1) % ncores] = 1.0
        in_maps.append(dict(x=x, x_halo=halo, sel=sel, **shared))
    return in_maps


def kernel(**inputs):
    T = 2048
    nc = build_program(T=T)
    in_maps = make_inmaps(inputs, T=T)
    res = run_bass_kernel_spmd(nc, in_maps, core_ids=list(range(NCORE)))
    B, L = 4, 4096
    out = np.empty((B, L, D), np.float32)
    for core in range(NCORE):
        b, h = core // 2, core % 2
        out[b, h * T:(h + 1) * T, :] = res.results[core]["out"]
    return out
